# version 17
# baseline (speedup 1.0000x reference)
import math
from contextlib import ExitStack

import numpy as np
import ml_dtypes

import concourse.bass as bass
import concourse.mybir as mybir
from concourse.bass_utils import run_bass_kernel_spmd

F32 = mybir.dt.float32
BF16 = mybir.dt.bfloat16
AF = mybir.ActivationFunctionType
ALU = mybir.AluOpType
AX = mybir.AxisListType

S = 4096
D = 1024
DQK = 128
DH = 4096
NTILE = 32
NT8 = 8
MU = math.sqrt(0.5)
SD = math.sqrt(0.25 * math.pi)
DEN = SD * math.sqrt(2.0)
C_SCORE = 1.0 / (S * DEN)
K1 = math.sqrt(DQK) / DEN
CB = -MU / DEN
MASKVAL = -1.0e5
BLEN = 1152
EPS = 1e-5


def _bucket(n: int) -> int:
    if n < 16:
        return max(n, 0)
    nf = np.float32(max(n, 1))
    t = np.float32(np.log(nf / np.float32(16.0)))
    t = np.float32(t / np.float32(math.log(8.0)))
    v = np.float32(t * np.float32(16.0))
    return min(16 + int(np.int32(v)), 31)


def _emat_host() -> np.ndarray:
    e = np.zeros((32, BLEN), dtype=np.float32)
    for m in range(BLEN):
        n = m - 512
        b = _bucket(n) if n >= 0 else 0
        e[b, m] = 1.0
    return e


class Sem:
    def __init__(self, es, nc, name):
        self.h = es.enter_context(nc.semaphore(name))
        self.n = 0

    def inc(self, instr, k=1):
        instr.then_inc(self.h, k)
        self.n += k
        return self.n

    def dinc(self, instr):
        return self.inc(instr, 16)


def _w(eng, sem, val):
    if val > 0:
        eng.wait_ge(sem.h, val)


def build_program(debug=False, ntiles=8):
    nc = bass.Bass("TRN2", target_bir_lowering=False)
    dbg_kind = "ExternalOutput" if debug else "Internal"

    x_d = nc.dram_tensor("x", [S, D], F32, kind="ExternalInput")
    wqk_d = nc.dram_tensor("wqk", [D, DQK], F32, kind="ExternalInput")
    bqk_d = nc.dram_tensor("bqk", [DQK], F32, kind="ExternalInput")
    gam_d = nc.dram_tensor("gamma", [2, DQK], F32, kind="ExternalInput")
    bet_d = nc.dram_tensor("beta", [2, DQK], F32, kind="ExternalInput")
    wv_d = nc.dram_tensor("wv", [D, D], F32, kind="ExternalInput")
    bv_d = nc.dram_tensor("bv", [D], F32, kind="ExternalInput")
    relb_d = nc.dram_tensor("relb", [32], F32, kind="ExternalInput")
    emat_d = nc.dram_tensor("emat", [32, BLEN], F32, kind="ExternalInput")
    w1t_d = nc.dram_tensor("w1t", [D, DH], F32, kind="ExternalInput")
    w2t_d = nc.dram_tensor("w2t", [D, DH], F32, kind="ExternalInput")
    h1t_d = nc.dram_tensor("h1t", [DH, DH], F32, kind="ExternalInput")
    h2t_d = nc.dram_tensor("h2t", [DH, DH], F32, kind="ExternalInput")
    w3t_d = nc.dram_tensor("w3t", [DH, D], F32, kind="ExternalInput")
    w1s_d = nc.dram_tensor("w1s", [DH], F32, kind="ExternalInput")
    w2s_d = nc.dram_tensor("w2s", [DH], F32, kind="ExternalInput")
    h1s_d = nc.dram_tensor("h1s", [DH], F32, kind="ExternalInput")
    h2s_d = nc.dram_tensor("h2s", [DH], F32, kind="ExternalInput")
    w3s_d = nc.dram_tensor("w3s", [D], F32, kind="ExternalInput")
    id_d = nc.dram_tensor("ident", [128, 128], BF16, kind="ExternalInput")
    out_d = nc.dram_tensor("out", [S, D], F32, kind="ExternalOutput")

    w1b = nc.dram_tensor("w1b", [D, DH], BF16, kind="Internal")
    w2b = nc.dram_tensor("w2b", [D, DH], BF16, kind="Internal")
    h1b = nc.dram_tensor("h1b", [DH, DH], BF16, kind="Internal")
    h2b = nc.dram_tensor("h2b", [DH, DH], BF16, kind="Internal")
    w3b = nc.dram_tensor("w3b", [DH, D], BF16, kind="Internal")
    xnd = nc.dram_tensor("xnd", [NT8, 128, 4096], BF16, kind=dbg_kind)
    sd_d = nc.dram_tensor("sd", [S, D], F32, kind=dbg_kind)
    bvd = nc.dram_tensor("bvd", [BLEN], F32, kind=dbg_kind)
    toepd = nc.dram_tensor("toepd", [5, 128, 512], F32, kind=dbg_kind)
    if debug:
        dbgst_d = nc.dram_tensor("dbgst", [6, 128, 32], F32, kind="ExternalOutput")
        dbgq_d = nc.dram_tensor("dbgq", [128, S], BF16, kind="ExternalOutput")
        dbgk_d = nc.dram_tensor("dbgk", [128, S], BF16, kind="ExternalOutput")
        dbgv_d = nc.dram_tensor("dbgv", [128, 32 * 1024], BF16, kind="ExternalOutput")
        dbgw_d = nc.dram_tensor("dbgw", [128, 8 * 1024], BF16, kind="ExternalOutput")
        dbgh_d = nc.dram_tensor("dbgh", [128, 32 * 512], BF16, kind="ExternalOutput")

    es = ExitStack()
    with es:
        sems = {}

        def SM(name):
            s = Sem(es, nc, name)
            sems[name] = s
            return s

        Lc = SM("Lc")
        Lb = SM("Lb")
        Lwp = [SM("Lwp0"), SM("Lwp1")]
        Owp = [SM("Owp0"), SM("Owp1")]
        Gp = SM("Gp")
        V0 = SM("V0")
        P0 = SM("P0")
        Lx1p = [SM("Lx1a"), SM("Lx1b")]
        V1 = SM("V1")
        S1 = SM("S1")
        Lx2p = [SM("Lx2a"), SM("Lx2b")]
        V2 = SM("V2")
        S2 = SM("S2")
        P2 = SM("P2")
        G2p = [SM("G2a"), SM("G2b")]
        V3 = SM("V3")
        S3 = SM("S3")
        P3 = SM("P3")
        P3a = SM("P3a")
        G3p = [SM("G3a"), SM("G3b")]
        Lwb = [SM("Lwb0"), SM("Lwb1")]
        Lws4 = [SM(f"Lws{i}") for i in range(12)]
        Lww = SM("Lww")
        Lxn = SM("Lxn")
        Lsxp = [SM("Lsxa"), SM("Lsxb")]
        V4 = SM("V4")
        S4 = SM("S4")
        P4 = SM("P4")
        Gop = [SM("Goa"), SM("Gob")]
        Lfp = [SM(f"Lfp{i}") for i in range(8)]
        Owfp = [SM("Owf0"), SM("Owf1")]
        Owsp = [SM(f"Ows{i}") for i in range(12)]

        block = es.enter_context(nc.Block())

        def barrier():
            snap = [(s.h, s.n) for s in sems.values() if s.n > 0]
            for engf in (block.sync, block.tensor, block.scalar, block.vector,
                         block.gpsimd):
                @engf
                def _(eng, snap=snap):
                    for h, n in snap:
                        eng.wait_ge(h, n)

        epsv = es.enter_context(nc.sbuf_tensor("epsv", [128, 1], F32))
        sums = es.enter_context(nc.sbuf_tensor("sums", [128, 32], F32))
        sumsq = es.enter_context(nc.sbuf_tensor("sumsq", [128, 32], F32))
        mean_a = es.enter_context(nc.sbuf_tensor("mean_a", [128, 32], F32))
        var_a = es.enter_context(nc.sbuf_tensor("var_a", [128, 32], F32))
        std_a = es.enter_context(nc.sbuf_tensor("std_a", [128, 32], F32))
        rstd_a = es.enter_context(nc.sbuf_tensor("rstd_a", [128, 32], F32))
        sid = es.enter_context(nc.sbuf_tensor("sid", [128, 128], BF16))

        esA = ExitStack()
        with esA:
            Q = esA.enter_context(nc.sbuf_tensor("Q", [128, S], BF16))
            K = esA.enter_context(nc.sbuf_tensor("K", [128, S], BF16))
            Vt = esA.enter_context(nc.sbuf_tensor("Vt", [128, 32 * 1024], BF16))
            toep5 = esA.enter_context(nc.sbuf_tensor("toep5", [128, 5 * 512], F32))
            b31v = esA.enter_context(nc.sbuf_tensor("b31v", [128, 1], F32))
            bqkv = esA.enter_context(nc.sbuf_tensor("bqkv", [128, 1], F32))
            g0v = esA.enter_context(nc.sbuf_tensor("g0v", [128, 1], F32))
            b0v = esA.enter_context(nc.sbuf_tensor("b0v", [128, 1], F32))
            g1v = esA.enter_context(nc.sbuf_tensor("g1v", [128, 1], F32))
            b1v = esA.enter_context(nc.sbuf_tensor("b1v", [128, 1], F32))
            g0c = esA.enter_context(nc.sbuf_tensor("g0c", [128, 1], F32))
            b0c = esA.enter_context(nc.sbuf_tensor("b0c", [128, 1], F32))

            es1ab = ExitStack()
            wqk_sb = es1ab.enter_context(nc.sbuf_tensor("wqk_sb", [128, 1024], BF16))
            wvb = es1ab.enter_context(nc.sbuf_tensor("wvb", [128, 8 * 1024], BF16))
            bvbc = es1ab.enter_context(nc.sbuf_tensor("bvbc", [128, 1024], F32))
            xa = [es1ab.enter_context(nc.sbuf_tensor(f"xa{i}", [128, 1024], F32))
                  for i in range(2)]

            es0 = ExitStack()
            with es0, nc.named_scope("ph0_prep"):
                swf = [es0.enter_context(nc.sbuf_tensor(f"swf{i}", [128, 1024], F32))
                       for i in range(2)]
                relb_t = es0.enter_context(nc.sbuf_tensor("relb_t", [32, 1], F32))
                emat_t = es0.enter_context(nc.sbuf_tensor("emat_t", [32, BLEN], F32))
                bvec_sb = es0.enter_context(nc.sbuf_tensor("bvec_sb", [1, BLEN], F32))
                pb = es0.enter_context(nc.psum_tensor("pb", [1, 512], F32))

                @block.sync
                def _(sync):
                    Lc.dinc(sync.dma_start(sid[:, :], id_d[:, :]))
                    Lc.dinc(sync.dma_start(
                        bqkv[:, :], bass.AP(bqk_d, 0, [[1, 128], [1, 1]])))
                    Lc.dinc(sync.dma_start(
                        g0v[:, :], bass.AP(gam_d, 0, [[1, 128], [1, 1]])))
                    Lc.dinc(sync.dma_start(
                        b0v[:, :], bass.AP(bet_d, 0, [[1, 128], [1, 1]])))
                    Lc.dinc(sync.dma_start(
                        g1v[:, :], bass.AP(gam_d, 128, [[1, 128], [1, 1]])))
                    Lc.dinc(sync.dma_start(
                        b1v[:, :], bass.AP(bet_d, 128, [[1, 128], [1, 1]])))
                    Lc.dinc(sync.dma_start(
                        relb_t[:, :], bass.AP(relb_d, 0, [[1, 32], [1, 1]])))
                    Lc.dinc(sync.dma_start(emat_t[:, :], emat_d[:, :]))
                    Lc.dinc(sync.dma_start(
                        bvbc[:, :], bass.AP(bv_d, 0, [[0, 128], [1, 1024]])))
                    for ic in range(8):
                        Lc.dinc(sync.dma_start(
                            bass.AP(swf[0], ic * 128, [[1024, 128], [1, 128]]),
                            wqk_d[ic * 128:(ic + 1) * 128, :]))

                bp_chunks = [(0, 512), (512, 512), (1024, 128)]

                @block.vector
                def _(vector):
                    _w(vector, Lc, Lc.n)
                    vector.memset(epsv[:, :], EPS)
                    vector.tensor_scalar(g0c[:, :], g0v[:, :], C_SCORE, None, ALU.mult)
                    vector.tensor_scalar(b0c[:, :], b0v[:, :], C_SCORE, None, ALU.mult)
                    ins = vector.tensor_copy(wqk_sb[:, :], swf[0][:, 0:1024])
                    V0.inc(ins)

                for ci, (off, w) in enumerate(bp_chunks):
                    @block.tensor
                    def _(tensor, ci=ci, off=off, w=w):
                        if ci == 0:
                            _w(tensor, Lc, Lc.n)
                        else:
                            _w(tensor, V0, V0.n)
                        ins = tensor.matmul(pb[:, 0:w], relb_t[:, :],
                                            emat_t[:, off:off + w],
                                            start=True, stop=True)
                        P0.inc(ins)

                    @block.vector
                    def _(vector, ci=ci, off=off, w=w):
                        _w(vector, P0, P0.n)
                        ins = vector.tensor_scalar(bvec_sb[0:1, off:off + w],
                                                   pb[0:1, 0:w], K1, CB,
                                                   ALU.mult, ALU.add)
                        V0.inc(ins)

                @block.vector
                def _(vector):
                    vector.wait_ge(V0.h, V0.n)
                    ins = vector.memset(bvec_sb[0:1, 0:512], MASKVAL)
                    V0.inc(ins)
                v0_bvec = V0.n

                @block.gpsimd
                def _(gpsimd):
                    _w(gpsimd, V0, v0_bvec)
                    Gp.dinc(gpsimd.dma_start(
                        bass.AP(bvd, 0, [[BLEN, 1], [1, BLEN]]),
                        bvec_sb[0:1, :]))
                    gpsimd.wait_ge(Gp.h, 16)
                    Gp.dinc(gpsimd.dma_start(
                        bass.AP(toepd, 0, [[128 * 512, 5], [512, 128], [1, 512]]),
                        bass.AP(bvd, 128, [[128, 5], [-1, 128], [1, 512]])))
                    gpsimd.wait_ge(Gp.h, 32)
                    for t in range(5):
                        Gp.dinc(gpsimd.dma_start(
                            toep5[:, t * 512:(t + 1) * 512], toepd[t, :, :]))
                    Gp.dinc(gpsimd.dma_start(
                        b31v[:, :], bass.AP(bvd, BLEN - 1, [[0, 128], [1, 1]])))

                slabs = []
                for ic in range(8):
                    slabs.append(("wv", None, ic, 1024, None))

                cast_cnt = []
                out_cnt = []
                lw_cnt = []
                lb_cnt = {}
                cur_m = -1
                for k, (srcd, dstd, r, ncols, sc) in enumerate(slabs):
                    need_bcast = sc is not None and sc[0] != cur_m

                    @block.sync
                    def _(sync, k=k, srcd=srcd, r=r, ncols=ncols, sc=sc,
                          need_bcast=need_bcast):
                        if need_bcast:
                            if k >= 1:
                                _w(sync, V0, cast_cnt[k - 1])
                            Lb.dinc(sync.dma_start(
                                sbcast[:, 0:ncols],
                                bass.AP(sc[1], 0, [[0, 128], [1, ncols]])))
                        if k >= 2:
                            _w(sync, V0, cast_cnt[k - 2])
                        elif k == 0:
                            _w(sync, V0, 1)
                        src_ap = (wv_d[r * 128:(r + 1) * 128, :]
                                  if srcd == "wv" else
                                  srcd[r * 128:(r + 1) * 128, :])
                        Lwp[k % 2].dinc(sync.dma_start(swf[k % 2][:, 0:ncols],
                                                       src_ap))
                        lw_cnt.append(Lwp[k % 2].n)

                    if need_bcast:
                        cur_m = sc[0]

                    @block.vector
                    def _(vector, k=k, srcd=srcd, r=r, ncols=ncols, sc=sc):
                        _w(vector, Lwp[k % 2], lw_cnt[k])
                        if sc is not None:
                            _w(vector, Lb, 16 * (sc[0] + 1))
                        if srcd == "wv":
                            ins = vector.tensor_copy(
                                wvb[:, r * 1024:(r + 1) * 1024],
                                swf[k % 2][:, 0:1024])
                        else:
                            if k >= 2:
                                _w(vector, Owp[k % 2], out_cnt[k - 2])
                            ins = vector.tensor_tensor(
                                sbf[k % 2][:, 0:ncols], swf[k % 2][:, 0:ncols],
                                sbcast[:, 0:ncols], ALU.mult)
                        V0.inc(ins)
                        cast_cnt.append(V0.n)

                    @block.gpsimd
                    def _(gpsimd, k=k, srcd=srcd, dstd=dstd, r=r, ncols=ncols):
                        if srcd == "wv":
                            out_cnt.append(Owp[k % 2].n)
                            return
                        _w(gpsimd, V0, cast_cnt[k])
                        Owp[k % 2].dinc(gpsimd.dma_start(
                            dstd[r * 128:(r + 1) * 128, :],
                            sbf[k % 2][:, 0:ncols]))
                        out_cnt.append(Owp[k % 2].n)

            barrier()


            es1 = ExitStack()
            with es1, nc.named_scope("ph1a_stats"):
                sq = es1.enter_context(nc.sbuf_tensor("sq", [128, 1024], F32))
                v1_tile = []
                for st in range(NTILE):
                    @block.sync
                    def _(sync, st=st):
                        if st >= 2:
                            _w(sync, V1, v1_tile[st - 2])
                        Lx1p[st % 2].dinc(sync.dma_start(
                            xa[st % 2][:, :], x_d[st * 128:(st + 1) * 128, :]))

                    @block.vector
                    def _(vector, st=st):
                        _w(vector, Lx1p[st % 2], 16 * (st // 2 + 1))
                        vector.tensor_tensor(sq[:, :], xa[st % 2][:, :],
                                             xa[st % 2][:, :], ALU.mult)
                        vector.reduce_sum(sums[:, st:st + 1], xa[st % 2][:, :],
                                          axis=AX.X)
                        ins = vector.reduce_sum(sumsq[:, st:st + 1], sq[:, :],
                                                axis=AX.X)
                        V1.inc(ins)
                        v1_tile.append(V1.n)

                @block.vector
                def _(vector):
                    ins = vector.tensor_scalar(mean_a[:, :], sums[:, :], 1.0 / D,
                                               None, ALU.mult)
                    V1.inc(ins)
                    vector.wait_ge(V1.h, V1.n)
                    ins = vector.tensor_tensor(sums[:, :], mean_a[:, :],
                                               mean_a[:, :], ALU.mult)
                    V1.inc(ins)
                    ins = vector.tensor_scalar(std_a[:, :], sumsq[:, :], 1.0 / D,
                                               None, ALU.mult)
                    V1.inc(ins)
                    vector.wait_ge(V1.h, V1.n)
                    ins = vector.tensor_tensor(var_a[:, :], std_a[:, :],
                                               sums[:, :], ALU.subtract)
                    V1.inc(ins)
                v1_var = V1.n

                @block.scalar
                def _(scalar):
                    _w(scalar, V1, v1_var)
                    ins = scalar.activation(std_a[:, :], var_a[:, :], AF.Sqrt,
                                            bias=epsv[:, :])
                    S1.inc(ins)

                @block.vector
                def _(vector):
                    _w(vector, S1, S1.n)
                    ins = vector.reciprocal(rstd_a[:, :], std_a[:, :])
                    V1.inc(ins)

            barrier()

            es1b = ExitStack()
            with es1b, nc.named_scope("ph1b_qkv"):
                xn_tm = [es1b.enter_context(
                    nc.sbuf_tensor(f"xn_tm{i}", [128, 1024], BF16)) for i in range(2)]
                xnf = [es1b.enter_context(nc.sbuf_tensor(f"xnf{i}", [128, 4096], BF16))
                       for i in range(2)]
                qkt = es1b.enter_context(nc.sbuf_tensor("qkt", [128, 512], F32))
                vtmp = [es1b.enter_context(
                    nc.sbuf_tensor(f"vtmp{i}", [128, 512], F32)) for i in range(8)]
                ptp = [es1b.enter_context(
                    nc.psum_tensor(f"ptp{i}", [128, 512], BF16)) for i in range(2)]
                pqk = es1b.enter_context(nc.psum_tensor("pqk", [128, 512], F32))
                pv = [es1b.enter_context(
                    nc.psum_tensor(f"pv{i}", [128, 512], F32)) for i in range(2)]

                ln_cnt = []
                tr_cnt = []
                ev_cnt = []
                vadd_cnt = []
                vsil_cnt = []
                pv_cnt = []
                qks_cnt = []
                qkaff_cnt = []
                xst_cnt = []

                for t8 in range(NT8):
                    for sub in range(4):
                        g = t8 * 4 + sub

                        @block.sync
                        def _(sync, g=g):
                            if g >= 2:
                                _w(sync, V2, ln_cnt[g - 2])
                            Lx2p[g % 2].dinc(sync.dma_start(
                                xa[g % 2][:, :], x_d[g * 128:(g + 1) * 128, :]))

                        @block.vector
                        def _(vector, g=g, sub=sub, t8=t8):
                            _w(vector, Lx2p[g % 2], 16 * (g // 2 + 1))
                            if g >= 2:
                                _w(vector, P2, tr_cnt[g - 2])
                            ins = vector.tensor_scalar(
                                xn_tm[g % 2][:, :], xa[g % 2][:, :],
                                mean_a[:, g:g + 1], rstd_a[:, g:g + 1],
                                ALU.subtract, ALU.mult)
                            V2.inc(ins)
                            ln_cnt.append(V2.n)

                        @block.tensor
                        def _(tensor, g=g, sub=sub, t8=t8):
                            _w(tensor, V2, ln_cnt[g])
                            if g >= 1:
                                _w(tensor, V2, ev_cnt[g - 1])
                            for ic in range(8):
                                ins = tensor.transpose(
                                    ptp[ic // 4][:, (ic % 4) * 128:(ic % 4 + 1) * 128],
                                    xn_tm[g % 2][:, ic * 128:(ic + 1) * 128],
                                    sid[:, :])
                                if ic == 3 or ic == 7:
                                    P2.inc(ins)
                            tr_cnt.append(P2.n)

                        @block.vector
                        def _(vector, g=g, sub=sub, t8=t8):
                            if t8 >= 2 and sub == 0:
                                _w(vector, xst_cnt[t8 - 2][0], xst_cnt[t8 - 2][1])
                                _w(vector, P2, pv_cnt[(t8 - 2) * 8 + 7])
                            for h in range(2):
                                _w(vector, P2, tr_cnt[g] - 1 + h)
                                ins = vector.tensor_copy(
                                    bass.AP(xnf[t8 % 2], sub * 128 + h * 4 * 512,
                                            [[4096, 128], [512, 4], [1, 128]]),
                                    ptp[h][:, :])
                                V2.inc(ins)
                            ev_cnt.append(V2.n)

                    @block.tensor
                    def _(tensor, t8=t8):
                        _w(tensor, V2, ev_cnt[t8 * 4 + 3])
                        if t8 >= 1:
                            _w(tensor, S2, qks_cnt[t8 - 1])
                        for ic in range(8):
                            ins = tensor.matmul(pqk[:, :],
                                                wqk_sb[:, ic * 128:(ic + 1) * 128],
                                                xnf[t8 % 2][:, ic * 512:(ic + 1) * 512],
                                                start=(ic == 0), stop=(ic == 7))
                        P2.inc(ins)
                    p2_qk = P2.n

                    @block.scalar
                    def _(scalar, t8=t8, p2_qk=p2_qk):
                        _w(scalar, P2, p2_qk)
                        if t8 >= 1:
                            _w(scalar, V2, qkaff_cnt[t8 - 1])
                        ins = scalar.activation(qkt[:, :], pqk[:, :], AF.Silu,
                                                bias=bqkv[:, :])
                        S2.inc(ins)
                        qks_cnt.append(S2.n)

                    @block.vector
                    def _(vector, t8=t8):
                        _w(vector, S2, qks_cnt[t8])
                        vector.tensor_scalar(Q[:, t8 * 512:(t8 + 1) * 512],
                                             qkt[:, :], g0c[:, :], b0c[:, :],
                                             ALU.mult, ALU.add)
                        ins = vector.tensor_scalar(K[:, t8 * 512:(t8 + 1) * 512],
                                                   qkt[:, :], g1v[:, :], b1v[:, :],
                                                   ALU.mult, ALU.add)
                        V2.inc(ins)
                        qkaff_cnt.append(V2.n)

                    for sub in range(4):
                        for oc2 in range(2):
                            gi = t8 * 8 + sub * 2 + oc2

                            @block.tensor
                            def _(tensor, t8=t8, sub=sub, oc2=oc2, gi=gi):
                                if gi >= 2:
                                    _w(tensor, V2, vadd_cnt[gi - 2])
                                for ic in range(8):
                                    ins = tensor.matmul(
                                        pv[gi % 2][:, :],
                                        xnf[t8 % 2][:, ic * 512 + sub * 128:
                                            ic * 512 + (sub + 1) * 128],
                                        wvb[:, ic * 1024 + oc2 * 512:
                                            ic * 1024 + (oc2 + 1) * 512],
                                        start=(ic == 0), stop=(ic == 7))
                                P2.inc(ins)
                                pv_cnt.append(P2.n)

                            @block.vector
                            def _(vector, gi=gi, oc2=oc2):
                                _w(vector, P2, pv_cnt[gi])
                                if gi >= 8:
                                    _w(vector, S2, vsil_cnt[gi - 8])
                                ins = vector.tensor_tensor(
                                    vtmp[gi % 8][:, :], pv[gi % 2][:, :],
                                    bvbc[:, oc2 * 512:(oc2 + 1) * 512], ALU.add)
                                V2.inc(ins)
                                vadd_cnt.append(V2.n)

                    @block.scalar
                    def _(scalar, t8=t8):
                        for sub in range(4):
                            for oc2 in range(2):
                                gi = t8 * 8 + sub * 2 + oc2
                                _w(scalar, V2, vadd_cnt[gi])
                                jc = t8 * 4 + sub
                                ins = scalar.activation(
                                    Vt[:, jc * 1024 + oc2 * 512:
                                       jc * 1024 + (oc2 + 1) * 512],
                                    vtmp[gi % 8][:, :], AF.Silu)
                                S2.inc(ins)
                                vsil_cnt.append(S2.n)

                    @block.gpsimd
                    def _(gpsimd, t8=t8):
                        _w(gpsimd, S2, vsil_cnt[t8 * 8 + 7])
                        G2p[t8 % 2].dinc(gpsimd.dma_start(
                            xnd[t8, :, :], xnf[t8 % 2][:, :]))
                        xst_cnt.append((G2p[t8 % 2], G2p[t8 % 2].n))

            es1ab.close()
            barrier()

            if debug:
                @block.gpsimd
                def _(gpsimd):
                    for h, n in [(sm.h, sm.n) for sm in sems.values() if sm.n > 0]:
                        gpsimd.wait_ge(h, n)
                    for t, src in [(0, sums), (1, sumsq), (2, mean_a),
                                   (3, var_a), (4, std_a), (5, rstd_a)]:
                        Gp.dinc(gpsimd.dma_start(dbgst_d[t, :, :], src[:, :]))
                    Gp.dinc(gpsimd.dma_start(dbgq_d[:, :], Q[:, :]))
                    Gp.dinc(gpsimd.dma_start(dbgk_d[:, :], K[:, :]))
                    Gp.dinc(gpsimd.dma_start(dbgv_d[:, :], Vt[:, :]))
                    Gp.dinc(gpsimd.dma_start(dbgw_d[:, :], wvb[:, :]))

            es2 = ExitStack()
            with es2, nc.named_scope("ph2_attn"):
                attst = [es2.enter_context(
                    nc.sbuf_tensor(f"attst{i}", [128, 32 * 512], BF16))
                    for i in range(2)]
                t1w = es2.enter_context(nc.sbuf_tensor("t1w", [128, 2048], F32))
                e4 = es2.enter_context(nc.sbuf_tensor("e4", [128, 2048], F32))
                sstg = [es2.enter_context(
                    nc.sbuf_tensor(f"sstg{i}", [128, 1024], F32)) for i in range(2)]
                psc = [es2.enter_context(
                    nc.psum_tensor(f"psc{i}", [128, 512], F32)) for i in range(2)]
                pav = [es2.enter_context(
                    nc.psum_tensor(f"pav{i}", [128, 512], F32)) for i in range(4)]

                NJ = [4 * i + 4 for i in range(8)]
                sc_idx = 0
                psc_free = []
                t1w_free = (None, 0)
                e4_free = (None, 0)
                aff_cnt = {}
                aff_blk = {}
                erf_cnt = {}
                av_done = []
                evac_cnt = {}
                store_cnt = {}
                pav_free = [(None, 0)] * 4
                sstg_free = [(None, 0)] * 2

                def emit_scores(I):
                    nonlocal sc_idx, t1w_free, e4_free
                    nj = NJ[I]
                    for Jb in range(nj // 4):
                        for j4 in range(4):
                            Jc = Jb * 4 + j4
                            gidx = sc_idx
                            sc_idx += 1
                            off = 512 * I - 128 * Jc
                            near = off <= 128
                            par = gidx % 2

                            @block.tensor
                            def _(tensor, I=I, Jc=Jc, gidx=gidx, par=par):
                                if gidx >= 2:
                                    h, n = psc_free[gidx - 2]
                                    _w(tensor, h, n)
                                ins = tensor.matmul(
                                    psc[par][:, :],
                                    K[:, Jc * 128:(Jc + 1) * 128],
                                    Q[:, I * 512:(I + 1) * 512],
                                    start=True, stop=True)
                                P3.inc(ins)
                            p3_here = P3.n

                            @block.vector
                            def _(vector, I=I, Jc=Jc, j4=j4, par=par,
                                  near=near, p3_here=p3_here):
                                _w(vector, P3, p3_here)
                                if j4 == 0:
                                    h, n = t1w_free
                                    if h is not None:
                                        _w(vector, h, n)
                                dst = t1w[:, j4 * 512:(j4 + 1) * 512]
                                if near:
                                    tidx = 4 * I - Jc + 3
                                    ins = vector.tensor_tensor(
                                        dst, psc[par][:, :],
                                        toep5[:, tidx * 512:(tidx + 1) * 512],
                                        ALU.add)
                                else:
                                    ins = vector.tensor_scalar(
                                        dst, psc[par][:, :], b31v[:, :], None,
                                        ALU.add)
                                V3.inc(ins)
                            psc_free.append((V3, V3.n))

                        @block.scalar
                        def _(scalar, I=I, Jb=Jb):
                            _w(scalar, V3, V3.n)
                            h, n = e4_free
                            if h is not None:
                                _w(scalar, h, n)
                            ins = scalar.activation(e4[:, :], t1w[:, :], AF.Erf)
                            S3.inc(ins)
                        t1w_free = (S3, S3.n)

                        @block.vector
                        def _(vector, I=I, Jb=Jb):
                            _w(vector, S3, S3.n)
                            if I >= 2 and Jb == 0:
                                _w(vector, P3a, 8 * (I - 1))
                            ins = vector.tensor_scalar(
                                attst[I % 2][:, Jb * 2048:(Jb + 1) * 2048],
                                e4[:, :], 0.5, 0.5, ALU.mult, ALU.add)
                            V3.inc(ins)
                        e4_free = (V3, V3.n)
                        aff_blk[(I, Jb)] = V3.n
                    aff_cnt[I] = V3.n

                def emit_av(I):
                    nj = NJ[I]
                    for q in range(4):
                        for dv in range(2):
                            gidx = q * 2 + dv
                            bank = gidx % 4

                            @block.tensor
                            def _(tensor, I=I, q=q, dv=dv, gidx=gidx, bank=bank,
                                  nj=nj):
                                h, n = pav_free[bank]
                                if h is not None:
                                    _w(tensor, h, n)
                                for Jc in range(nj):
                                    if gidx == 0 and Jc % 4 == 0:
                                        _w(tensor, V3, aff_blk[(I, Jc // 4)])
                                    ins = tensor.matmul(
                                        pav[bank][:, :],
                                        attst[I % 2][:, Jc * 512 + q * 128:
                                                     Jc * 512 + (q + 1) * 128],
                                        Vt[:, Jc * 1024 + dv * 512:
                                           Jc * 1024 + (dv + 1) * 512],
                                        start=(Jc == 0), stop=(Jc == nj - 1))
                                P3a.inc(ins)
                            p3a_here = P3a.n

                            @block.vector
                            def _(vector, I=I, q=q, dv=dv, bank=bank,
                                  p3a_here=p3a_here):
                                _w(vector, P3a, p3a_here)
                                if dv == 0:
                                    h, n = sstg_free[q % 2]
                                    if h is not None:
                                        _w(vector, h, n)
                                ins = vector.tensor_copy(
                                    sstg[q % 2][:, dv * 512:(dv + 1) * 512],
                                    pav[bank][:, :])
                                V3.inc(ins)
                            pav_free[bank] = (V3, V3.n)
                            evac_cnt[(I, gidx)] = V3.n

                        @block.gpsimd
                        def _(gpsimd, I=I, q=q):
                            _w(gpsimd, V3, evac_cnt[(I, q * 2 + 1)])
                            G3p[q % 2].dinc(gpsimd.dma_start(
                                sd_d[I * 512 + q * 128:I * 512 + (q + 1) * 128, :],
                                sstg[q % 2][:, :]))
                        sstg_free[q % 2] = (G3p[q % 2], G3p[q % 2].n)

                for I in range(8):
                    emit_scores(I)
                    if I >= 1:
                        emit_av(I - 1)
                emit_av(7)

            barrier()

        esB = ExitStack()
        with esB, nc.named_scope("ph3_ffn"):
            xnf3 = esB.enter_context(nc.sbuf_tensor("xnf3", [128, 4096], BF16))
            hA = esB.enter_context(nc.sbuf_tensor("hA", [128, 32 * 512], BF16))
            hB = esB.enter_context(nc.sbuf_tensor("hB", [128, 32 * 512], BF16))
            wb1 = [esB.enter_context(nc.sbuf_tensor(f"wb1_{i}", [128, 4096], BF16))
                   for i in range(2)]
            wb2 = [esB.enter_context(nc.sbuf_tensor(f"wb2_{i}", [128, 4096], BF16))
                   for i in range(2)]
            wsm = [esB.enter_context(nc.sbuf_tensor(f"wsm{i}", [128, 1024], BF16))
                   for i in range(12)]
            tsil = [esB.enter_context(nc.sbuf_tensor(f"tsil{i}", [128, 512], F32))
                    for i in range(4)]
            gts4 = esB.enter_context(nc.sbuf_tensor("gts4", [128, 4096], F32))
            s_t = [esB.enter_context(nc.sbuf_tensor(f"s_t{i}", [128, 1024], F32))
                   for i in range(2)]
            cmb = esB.enter_context(nc.sbuf_tensor("cmb", [128, 1024], F32))
            x_t = [esB.enter_context(nc.sbuf_tensor(f"x_t{i}", [128, 1024], F32))
                   for i in range(2)]
            swf3 = [esB.enter_context(nc.sbuf_tensor(f"swf3_{i}", [128, 1024], F32))
                    for i in range(5)]
            sbc1p = [esB.enter_context(nc.sbuf_tensor(f"sbc1_{i}", [128, 1024], F32))
                     for i in range(2)]
            sbc2p = [esB.enter_context(nc.sbuf_tensor(f"sbc2_{i}", [128, 512], F32))
                     for i in range(2)]
            ps8 = [esB.enter_context(nc.psum_tensor(f"ps8_{i}", [128, 512], F32))
                   for i in range(8)]


            bank_free = [(None, 0)] * 8
            wsm_free = [(None, 0)] * 12
            wb_free = [(None, 0), (None, 0)]
            xnf3_free = (None, 0)
            hA_w_ok = (None, 0)
            hB_w_ok = (None, 0)
            tsil_free = [(None, 0)] * 4
            gts_free = (None, 0)
            sx_free = [(None, 0)] * 2
            st1_done = (None, 0)

            st0_cast = []
            sbc_war2 = [(None, 0), (None, 0)]
            pre = {}
            v4_mult = {}
            s23_mm = {}
            for tk in range(ntiles):
                s4b = S4.n
                p_sil1 = lambda g, o4, s4b=s4b: s4b + g * 4 + o4 + 1
                p_gel = lambda st, g, o8, s4b=s4b: s4b + 32 + st * 32 + g * 8 + o8 + 1
                p_sil4 = lambda q, od, s4b=s4b: s4b + 96 + q * 2 + od + 1
                if tk == 1:
                    snap0 = ([(h.h, h.n) for h in Owfp if h.n] +
                             [(h.h, h.n) for h in Owsp if h.n])
                    for engf in (block.sync, block.gpsimd):
                        @engf
                        def _(eng, snap0=snap0):
                            for hh, nn in snap0:
                                eng.wait_ge(hh, nn)
                if tk in pre:
                    lxn_here = pre[tk]["xn"]
                else:
                    @block.sync
                    def _(sync, tk=tk):
                        h, n = xnf3_free
                        if h is not None:
                            _w(sync, h, n)
                        Lxn.dinc(sync.dma_start(xnf3[:, :], xnd[tk, :, :]))
                    lxn_here = Lxn.n

                @block.gpsimd
                def _(gpsimd, tk=tk):
                    for q in range(2):
                        h, n = sx_free[q % 2]
                        if h is not None:
                            _w(gpsimd, h, n)
                        Lsxp[q % 2].dinc(gpsimd.dma_start(
                            s_t[q % 2][:, :],
                            sd_d[tk * 512 + q * 128:tk * 512 + (q + 1) * 128, :]))
                        Lsxp[q % 2].dinc(gpsimd.dma_start(
                            x_t[q % 2][:, :],
                            x_d[tk * 512 + q * 128:tk * 512 + (q + 1) * 128, :]))
                sx_here = [(Lsxp[0], Lsxp[0].n), (Lsxp[1], Lsxp[1].n)]

                s4b1 = S4.n
                st1_mult = {}
                st1_sil = {}
                for g in range(8):
                    if tk in pre and g in pre[tk]:
                        lw3_here = pre[tk][g]
                    elif tk == 0:
                        ch = [(w1t_d, w1b, wb1, ic) for ic in range(8)] + \
                             [(w2t_d, w2b, wb2, ic) for ic in range(8)]
                        base = len(st0_cast)
                        lf_cnt = []

                        @block.sync
                        def _(sync, g=g, ch=ch, base=base, lf=lf_cnt,
                              sbw=(sbc_war2[g % 2] if True else None),
                              v4_at=V4.n):
                            if sbw[0] is not None:
                                _w(sync, sbw[0], sbw[1])
                            Lb.dinc(sync.dma_start(
                                sbc1p[g % 2][:, 0:512],
                                bass.AP(w1s_d, g * 512, [[0, 128], [1, 512]])))
                            Lb.dinc(sync.dma_start(
                                sbc2p[g % 2][:, 0:512],
                                bass.AP(w2s_d, g * 512, [[0, 128], [1, 512]])))
                            for j, (srcd, dstd, wbx, ic) in enumerate(ch):
                                idx = base + j
                                if idx >= 5:
                                    if idx - 5 < base:
                                        _w(sync, V4, st0_cast[idx - 5])
                                    else:
                                        _w(sync, V4, v4_at + (j - 5) + 1)
                                Lfp[idx % 5].dinc(sync.dma_start(
                                    swf3[idx % 5][:, 0:512],
                                    srcd[ic * 128:(ic + 1) * 128,
                                         g * 512:(g + 1) * 512]))
                                lf.append(Lfp[idx % 5].n)
                        lb_here = Lb.n

                        @block.vector
                        def _(vector, g=g, ch=ch, base=base, lf=lf_cnt,
                              lb_here=lb_here):
                            h, n = wb_free[g % 2]
                            if h is not None:
                                _w(vector, h, n)
                            if g >= 2:
                                _w(vector, Owfp[g % 2], Owfp[g % 2].n)
                            _w(vector, Lb, lb_here)
                            for j, (srcd, dstd, wbx, ic) in enumerate(ch):
                                idx = base + j
                                _w(vector, Lfp[idx % 5], lf[j])
                                sb = (sbc1p if wbx is wb1 else sbc2p)[g % 2]
                                ins = vector.tensor_tensor(
                                    wbx[g % 2][:, ic * 512:(ic + 1) * 512],
                                    swf3[idx % 5][:, 0:512], sb[:, 0:512],
                                    ALU.mult)
                                V4.inc(ins)
                                st0_cast.append(V4.n)
                        sbc_war2[g % 2] = (V4, V4.n)

                        @block.gpsimd
                        def _(gpsimd, g=g, ch=ch, base=base):
                            for j, (srcd, dstd, wbx, ic) in enumerate(ch):
                                _w(gpsimd, V4, st0_cast[base + j])
                                Owfp[g % 2].dinc(gpsimd.dma_start(
                                    dstd[ic * 128:(ic + 1) * 128,
                                         g * 512:(g + 1) * 512],
                                    wbx[g % 2][:, ic * 512:(ic + 1) * 512]))
                        lw3_here = (V4, st0_cast[-1])
                    else:
                        @block.sync
                        def _(sync, tk=tk, g=g):
                            h, n = wb_free[g % 2]
                            if h is not None:
                                _w(sync, h, n)
                            for ic in range(8):
                                Lwb[g % 2].dinc(sync.dma_start(
                                    wb1[g % 2][:, ic * 512:(ic + 1) * 512],
                                    w1b[ic * 128:(ic + 1) * 128,
                                        g * 512:(g + 1) * 512]))
                                Lwb[g % 2].dinc(sync.dma_start(
                                    wb2[g % 2][:, ic * 512:(ic + 1) * 512],
                                    w2b[ic * 128:(ic + 1) * 128,
                                        g * 512:(g + 1) * 512]))
                        lw3_here = (Lwb[g % 2], Lwb[g % 2].n)

                    @block.tensor
                    def _(tensor, tk=tk, g=g, lw3_here=lw3_here,
                          lxn_here=lxn_here):
                        _w(tensor, lw3_here[0], lw3_here[1])
                        if g == 0:
                            _w(tensor, Lxn, lxn_here)
                        for o4 in range(4):
                            h, n = bank_free[o4]
                            if h is not None:
                                _w(tensor, h, n)
                            h, n = bank_free[4 + o4]
                            if h is not None:
                                _w(tensor, h, n)
                        for t in range(14):
                            for o4 in range(4):
                                ic = t - 2 * o4
                                if ic < 0 or ic > 7:
                                    continue
                                ins = tensor.matmul(
                                    ps8[o4][:, :],
                                    wb1[g % 2][:, ic * 512 + o4 * 128:
                                               ic * 512 + (o4 + 1) * 128],
                                    xnf3[:, ic * 512:(ic + 1) * 512],
                                    start=(ic == 0), stop=(ic == 7))
                                if ic == 7:
                                    P4.inc(ins)
                                    st1_mult[(g, o4, "y1")] = P4.n
                                ins = tensor.matmul(
                                    ps8[4 + o4][:, :],
                                    wb2[g % 2][:, ic * 512 + o4 * 128:
                                               ic * 512 + (o4 + 1) * 128],
                                    xnf3[:, ic * 512:(ic + 1) * 512],
                                    start=(ic == 0), stop=(ic == 7))
                                if ic == 7:
                                    P4.inc(ins)
                                    st1_mult[(g, o4, "y2")] = P4.n
                    wb_free[g % 2] = (P4, P4.n)
                    if g == 7:
                        xnf3_free = (P4, P4.n)

                    @block.vector
                    def _(vector, g=g, tk=tk):
                        for o4 in range(4):
                            oc = g * 4 + o4
                            _w(vector, S4, s4b1 + g * 4 + o4 + 1)
                            _w(vector, P4, st1_mult[(g, o4, "y2")])
                            if g == 0 and o4 == 0:
                                h, n = hA_w_ok
                                if h is not None:
                                    _w(vector, h, n)
                            ins = vector.tensor_tensor(
                                hA[:, oc * 512:(oc + 1) * 512],
                                tsil[o4][:, :], ps8[4 + o4][:, :], ALU.mult)
                            V4.inc(ins)
                            bank_free[4 + o4] = (V4, V4.n)
                            v4_mult[(tk, g, o4)] = V4.n
                    for o4 in range(4):
                        bank_free[o4] = (S4, s4b1 + g * 4 + o4 + 1)
                st1_done = (V4, V4.n)

                if tk + 1 < ntiles:
                    pre[tk + 1] = {}

                    @block.gpsimd
                    def _(gpsimd, tk=tk):
                        h, n = xnf3_free
                        if h is not None:
                            _w(gpsimd, h, n)
                        Lxn.dinc(gpsimd.dma_start(xnf3[:, :], xnd[tk + 1, :, :]))
                    pre[tk + 1]["xn"] = Lxn.n

                    for g in range(2):
                        @block.gpsimd
                        def _(gpsimd, tk=tk, g=g):
                            if tk == 0:
                                for hh in Owfp:
                                    if hh.n:
                                        gpsimd.wait_ge(hh.h, hh.n)
                            h, n = wb_free[g % 2]
                            if h is not None:
                                _w(gpsimd, h, n)
                            for ic in range(8):
                                Lwb[g % 2].dinc(gpsimd.dma_start(
                                    wb1[g % 2][:, ic * 512:(ic + 1) * 512],
                                    w1b[ic * 128:(ic + 1) * 128,
                                        g * 512:(g + 1) * 512]))
                                Lwb[g % 2].dinc(gpsimd.dma_start(
                                    wb2[g % 2][:, ic * 512:(ic + 1) * 512],
                                    w2b[ic * 128:(ic + 1) * 128,
                                        g * 512:(g + 1) * 512]))
                        pre[tk + 1][g] = (Lwb[g % 2], Lwb[g % 2].n)

                s4b32 = s4b1 + 32
                p_gel = lambda st, g, o8, b=None: s4b32 + st * 32 + g * 8 + o8 + 1
                for stage, (wsrc, wsrcf, scv, hin, hout) in enumerate(
                        [(h1b, h1t_d, h1s_d, hA, hB),
                         (h2b, h2t_d, h2s_d, hB, hA)]):
                    for g in range(4):
                        blk_done = {}
                        if tk == 0:
                            @block.sync
                            def _(sync, scv=scv, g=g, sbw=sbc_war2[g % 2]):
                                if sbw[0] is not None:
                                    _w(sync, sbw[0], sbw[1])
                                Lb.dinc(sync.dma_start(
                                    sbc1p[g % 2][:, :],
                                    bass.AP(scv, g * 1024, [[0, 128], [1, 1024]])))
                            lb_here = Lb.n
                        for b in range(9):
                            tlo, thi = 4 * b, min(4 * b + 3, 34)
                            if b <= 7:
                                if tk == 0:
                                    base = len(st0_cast)
                                    lf_cnt = []

                                    @block.sync
                                    def _(sync, wsrcf=wsrcf, g=g, b=b, base=base,
                                          lf=lf_cnt):
                                        for j, ic in enumerate(range(4 * b, 4 * b + 4)):
                                            idx = base + j
                                            if idx >= 4:
                                                _w(sync, V4, st0_cast[idx - 4])
                                            Lfp[idx % 5].dinc(sync.dma_start(
                                                swf3[idx % 5][:, :],
                                                wsrcf[ic * 128:(ic + 1) * 128,
                                                      g * 1024:(g + 1) * 1024]))
                                            lf.append(Lfp[idx % 5].n)

                                    @block.vector
                                    def _(vector, g=g, b=b, base=base, lf=lf_cnt,
                                          lb_here=lb_here):
                                        _w(vector, Lb, lb_here)
                                        for j, ic in enumerate(range(4 * b, 4 * b + 4)):
                                            idx = base + j
                                            _w(vector, Lfp[idx % 5], lf[j])
                                            h, n = wsm_free[ic % 12]
                                            if h is not None:
                                                _w(vector, h, n)
                                            if Owsp[ic % 12].n:
                                                _w(vector, Owsp[ic % 12],
                                                   Owsp[ic % 12].n)
                                            ins = vector.tensor_tensor(
                                                wsm[ic % 12][:, :],
                                                swf3[idx % 5][:, :],
                                                sbc1p[g % 2][:, :], ALU.mult)
                                            V4.inc(ins)
                                            st0_cast.append(V4.n)
                                    sbc_war2[g % 2] = (V4, V4.n)

                                    @block.gpsimd
                                    def _(gpsimd, wsrc=wsrc, g=g, b=b, base=base):
                                        for j, ic in enumerate(range(4 * b, 4 * b + 4)):
                                            _w(gpsimd, V4, st0_cast[base + j])
                                            Owsp[ic % 12].dinc(gpsimd.dma_start(
                                                wsrc[ic * 128:(ic + 1) * 128,
                                                     g * 1024:(g + 1) * 1024],
                                                wsm[ic % 12][:, :]))
                                    lw3_blk = [(V4, st0_cast[base + j])
                                               for j in range(4)]
                                else:
                                    @block.sync
                                    def _(sync, wsrc=wsrc, g=g, b=b):
                                        for ic in range(4 * b, 4 * b + 4):
                                            h, n = wsm_free[ic % 12]
                                            if h is not None:
                                                _w(sync, h, n)
                                            Lws4[ic % 12].dinc(sync.dma_start(
                                                wsm[ic % 12][:, :],
                                                wsrc[ic * 128:(ic + 1) * 128,
                                                     g * 1024:(g + 1) * 1024]))
                                    lw3_blk = [(Lws4[(4 * b + j) % 12],
                                                Lws4[(4 * b + j) % 12].n)
                                               for j in range(4)]

                            @block.tensor
                            def _(tensor, stage=stage, g=g, b=b, hin=hin,
                                  tlo=tlo, thi=thi,
                                  lw3_blk=(lw3_blk if b <= 7 else None)):
                                if lw3_blk is not None:
                                    for h, n in lw3_blk:
                                        _w(tensor, h, n)
                                if b == 0 and g == 0:
                                    if stage == 0:
                                        h, n = st1_done
                                        _w(tensor, h, n)
                                if b == 0:
                                    for o8 in range(8):
                                        h, n = bank_free[o8]
                                        if h is not None:
                                            _w(tensor, h, n)
                                for t in range(tlo, thi + 1):
                                    for o8 in range(8):
                                        c = t - (o8 // 2)
                                        if c < 0 or c > 31:
                                            continue
                                        ins = tensor.matmul(
                                            ps8[o8][:, :],
                                            wsm[c % 12][:, o8 * 128:(o8 + 1) * 128],
                                            hin[:, c * 512:(c + 1) * 512],
                                            start=(c == 0), stop=(c == 31))
                                        if c == 31:
                                            P4.inc(ins)
                                            blk_done[o8] = P4.n
                                if b < 8:
                                    P4.inc(ins)
                            for t in range(tlo, thi + 1):
                                c = t - 3
                                if 0 <= c <= 31:
                                    wsm_free[c % 12] = (P4, P4.n)
                        for o8 in range(8):
                            s23_mm[(tk, stage, g, o8)] = blk_done[o8]
                            bank_free[o8] = (S4, p_gel(stage, g, o8))
                st3_done = (S4, 0)


                st4_mm = {}
                st4_sil = {}
                comb_cnt = {}
                if tk == 0:
                    @block.sync
                    def _(sync, sbw=sbc_war2[0]):
                        if sbw[0] is not None:
                            _w(sync, sbw[0], sbw[1])
                        Lb.dinc(sync.dma_start(
                            sbc1p[0][:, :],
                            bass.AP(w3s_d, 0, [[0, 128], [1, 1024]])))
                    lb_here = Lb.n
                for b4 in range(9):
                    tlo, thi = 4 * b4, min(4 * b4 + 3, 34)
                    if b4 <= 7:
                        if tk == 0:
                            base = len(st0_cast)
                            lf_cnt = []

                            @block.sync
                            def _(sync, b4=b4, base=base, lf=lf_cnt):
                                for j, ic in enumerate(range(4 * b4, 4 * b4 + 4)):
                                    idx = base + j
                                    if idx >= 5:
                                        _w(sync, V4, st0_cast[idx - 5])
                                    Lfp[idx % 5].dinc(sync.dma_start(
                                        swf3[idx % 5][:, :],
                                        w3t_d[ic * 128:(ic + 1) * 128, :]))
                                    lf.append(Lfp[idx % 5].n)

                            @block.vector
                            def _(vector, b4=b4, base=base, lf=lf_cnt,
                                  lb_here=lb_here):
                                _w(vector, Lb, lb_here)
                                for j, ic in enumerate(range(4 * b4, 4 * b4 + 4)):
                                    idx = base + j
                                    _w(vector, Lfp[idx % 5], lf[j])
                                    h, n = wsm_free[ic % 12]
                                    if h is not None:
                                        _w(vector, h, n)
                                    if Owsp[ic % 12].n:
                                        _w(vector, Owsp[ic % 12], Owsp[ic % 12].n)
                                    ins = vector.tensor_tensor(
                                        wsm[ic % 12][:, :],
                                        swf3[idx % 5][:, :], sbc1p[0][:, :],
                                        ALU.mult)
                                    V4.inc(ins)
                                    st0_cast.append(V4.n)
                            sbc_war2[0] = (V4, V4.n)

                            @block.gpsimd
                            def _(gpsimd, b4=b4, base=base):
                                for j, ic in enumerate(range(4 * b4, 4 * b4 + 4)):
                                    _w(gpsimd, V4, st0_cast[base + j])
                                    Owsp[ic % 12].dinc(gpsimd.dma_start(
                                        w3b[ic * 128:(ic + 1) * 128, :],
                                        wsm[ic % 12][:, :]))
                            lw4_blk = [(V4, st0_cast[base + j]) for j in range(4)]
                        else:
                            @block.sync
                            def _(sync, tk=tk, b4=b4):
                                for ic in range(4 * b4, 4 * b4 + 4):
                                    h, n = wsm_free[ic % 12]
                                    if h is not None:
                                        _w(sync, h, n)
                                    Lws4[ic % 12].dinc(sync.dma_start(
                                        wsm[ic % 12][:, :],
                                        w3b[ic * 128:(ic + 1) * 128, :]))
                            lw4_blk = [(Lws4[(4 * b4 + j) % 12],
                                        Lws4[(4 * b4 + j) % 12].n) for j in range(4)]

                    @block.tensor
                    def _(tensor, tk=tk, b4=b4, tlo=tlo, thi=thi,
                          lw4_blk=(lw4_blk if b4 <= 7 else None),
                          st3_done=st3_done):
                        if lw4_blk is not None:
                            for h, n in lw4_blk:
                                _w(tensor, h, n)
                        if b4 == 0:
                            _w(tensor, st3_done[0], st3_done[1])
                            for bank in range(8):
                                h, n = bank_free[bank]
                                if h is not None:
                                    _w(tensor, h, n)
                        for t in range(tlo, thi + 1):
                            for q in range(4):
                                for od in range(2):
                                    bank = q * 2 + od
                                    ic = t - (bank // 2)
                                    if ic < 0 or ic > 31:
                                        continue
                                    ins = tensor.matmul(
                                        ps8[bank][:, :],
                                        hA[:, ic * 512 + q * 128:
                                           ic * 512 + (q + 1) * 128],
                                        wsm[ic % 12][:, od * 512:(od + 1) * 512],
                                        start=(ic == 0), stop=(ic == 31))
                                    if ic == 31:
                                        P4.inc(ins)
                                        st4_mm[bank] = P4.n
                        if b4 < 8:
                            P4.inc(ins)
                    for t in range(tlo, thi + 1):
                        c = t - 3
                        if 0 <= c <= 31:
                            wsm_free[c % 12] = (P4, P4.n)

                s4q = s4b1 + 96
                gts_free0 = gts_free
                for q in range(4):
                    for od in range(2):
                        st4_sil[(q, od)] = s4q + q * 2 + od + 1
                        bank_free[q * 2 + od] = (S4, s4q + q * 2 + od + 1)

                for q in range(4):
                    if q >= 2:
                        @block.gpsimd
                        def _(gpsimd, tk=tk, q=q):
                            h, n = sx_free[q % 2]
                            if h is not None:
                                _w(gpsimd, h, n)
                            Lsxp[q % 2].dinc(gpsimd.dma_start(
                                s_t[q % 2][:, :],
                                sd_d[tk * 512 + q * 128:
                                     tk * 512 + (q + 1) * 128, :]))
                            Lsxp[q % 2].dinc(gpsimd.dma_start(
                                x_t[q % 2][:, :],
                                x_d[tk * 512 + q * 128:
                                    tk * 512 + (q + 1) * 128, :]))
                        sx_here.append((Lsxp[q % 2], Lsxp[q % 2].n))

                    @block.vector
                    def _(vector, tk=tk, q=q):
                        _w(vector, S4, st4_sil[(q, 1)])
                        _w(vector, sx_here[q][0], sx_here[q][1])
                        ins = vector.tensor_tensor(
                            cmb[:, :], gts4[:, q * 1024:(q + 1) * 1024],
                            s_t[q % 2][:, :], ALU.mult)
                        V4.inc(ins)
                        vector.wait_ge(V4.h, V4.n)
                        ins = vector.tensor_tensor(s_t[q % 2][:, :],
                                                   cmb[:, :],
                                                   x_t[q % 2][:, :], ALU.add)
                        V4.inc(ins)
                        comb_cnt[q] = V4.n

                    @block.gpsimd
                    def _(gpsimd, tk=tk, q=q):
                        _w(gpsimd, V4, comb_cnt[q])
                        Gop[q % 2].dinc(gpsimd.dma_start(
                            out_d[tk * 512 + q * 128:tk * 512 + (q + 1) * 128, :],
                            s_t[q % 2][:, :]))
                    sx_free[q % 2] = (Gop[q % 2], Gop[q % 2].n)
                gts_free = (V4, V4.n)

                @block.scalar
                def _(scalar, tk=tk):
                    for g in range(8):
                        for o4 in range(4):
                            _w(scalar, P4, st1_mult[(g, o4, "y1")])
                            if g >= 1:
                                _w(scalar, V4, v4_mult[(tk, g - 1, o4)])
                            elif tk >= 1:
                                _w(scalar, V4, v4_mult[(tk - 1, 7, o4)])
                            ins = scalar.activation(tsil[o4][:, :],
                                                    ps8[o4][:, :], AF.Silu)
                            S4.inc(ins)
                assert S4.n == s4b1 + 32
                @block.scalar
                def _(scalar, tk=tk, hbw=hB_w_ok):
                    for stage, hout in ((0, hB), (1, hA)):
                        for g in range(4):
                            for o8 in range(8):
                                oc = g * 8 + o8
                                _w(scalar, P4, s23_mm[(tk, stage, g, o8)])
                                if stage == 0 and g == 0 and o8 == 0:
                                    h, n = hbw
                                    if h is not None:
                                        _w(scalar, h, n)
                                ins = scalar.activation(
                                    hout[:, oc * 512:(oc + 1) * 512],
                                    ps8[o8][:, :], AF.Gelu)
                                S4.inc(ins)
                assert S4.n == s4b32 + 64

                @block.scalar
                def _(scalar, tk=tk, gfree=gts_free0):
                    for q in range(4):
                        for od in range(2):
                            bank = q * 2 + od
                            _w(scalar, P4, st4_mm[bank])
                            if q == 0 and od == 0:
                                h, n = gfree
                                if h is not None:
                                    _w(scalar, h, n)
                            ins = scalar.activation(
                                gts4[:, q * 1024 + od * 512:
                                     q * 1024 + (od + 1) * 512],
                                ps8[bank][:, :], AF.Silu)
                            S4.inc(ins)
                assert S4.n == s4q + 8

                hA_w_ok = (P4, P4.n)
                hB_w_ok = (P4, P4.n)

            if debug:
                @block.gpsimd
                def _(gpsimd):
                    for sm in sems.values():
                        if sm.n > 0:
                            gpsimd.wait_ge(sm.h, sm.n)
                    Gp.dinc(gpsimd.dma_start(dbgh_d[:, :], hA[:, :]))
                    gpsimd.wait_ge(Gp.h, Gp.n)

            @block.gpsimd
            def _(gpsimd):
                gpsimd.wait_ge(Gop[0].h, Gop[0].n)
                gpsimd.wait_ge(Gop[1].h, Gop[1].n)

    return nc


_CACHE = {}


def kernel(L0_attn_x, L0_attn_wqk, L0_attn_bqk, L0_attn_gamma, L0_attn_beta,
           L0_attn_wv, L0_attn_bv, L0_attn_rel_bias,
           L0_ffn_w1_w, L0_ffn_w1_s, L0_ffn_w2_w, L0_ffn_w2_s,
           L0_ffn_h1_w, L0_ffn_h1_s, L0_ffn_h2_w, L0_ffn_h2_s,
           L0_ffn_w3_w, L0_ffn_w3_s):
    if "nc" not in _CACHE:
        _CACHE["nc"] = build_program()
    nc = _CACHE["nc"]

    f32 = np.float32
    common = {
        "wqk": np.ascontiguousarray(L0_attn_wqk, dtype=f32),
        "bqk": np.ascontiguousarray(L0_attn_bqk, dtype=f32),
        "gamma": np.ascontiguousarray(L0_attn_gamma, dtype=f32),
        "beta": np.ascontiguousarray(L0_attn_beta, dtype=f32),
        "wv": np.ascontiguousarray(L0_attn_wv, dtype=f32),
        "bv": np.ascontiguousarray(L0_attn_bv, dtype=f32),
        "relb": np.ascontiguousarray(L0_attn_rel_bias, dtype=f32),
        "emat": _emat_host(),
        "w1t": np.ascontiguousarray(np.asarray(L0_ffn_w1_w, dtype=f32).T),
        "w2t": np.ascontiguousarray(np.asarray(L0_ffn_w2_w, dtype=f32).T),
        "h1t": np.ascontiguousarray(np.asarray(L0_ffn_h1_w, dtype=f32).T),
        "h2t": np.ascontiguousarray(np.asarray(L0_ffn_h2_w, dtype=f32).T),
        "w3t": np.ascontiguousarray(np.asarray(L0_ffn_w3_w, dtype=f32).T),
        "w1s": np.ascontiguousarray(L0_ffn_w1_s, dtype=f32),
        "w2s": np.ascontiguousarray(L0_ffn_w2_s, dtype=f32),
        "h1s": np.ascontiguousarray(L0_ffn_h1_s, dtype=f32),
        "h2s": np.ascontiguousarray(L0_ffn_h2_s, dtype=f32),
        "w3s": np.ascontiguousarray(L0_ffn_w3_s, dtype=f32),
        "ident": np.eye(128).astype(ml_dtypes.bfloat16),
    }
    x = np.asarray(L0_attn_x, dtype=f32)
    in_maps = []
    for c in range(8):
        m = dict(common)
        m["x"] = np.ascontiguousarray(x[c])
        in_maps.append(m)

    res = run_bass_kernel_spmd(nc, in_maps, core_ids=list(range(8)))
    return np.stack([r["out"] for r in res.results], axis=0)


# version 19
# speedup vs baseline: 1.0817x; 1.0817x over previous
import math
from contextlib import ExitStack

import numpy as np
import ml_dtypes

import concourse.bass as bass
import concourse.mybir as mybir
from concourse.bass_utils import run_bass_kernel_spmd

F32 = mybir.dt.float32
BF16 = mybir.dt.bfloat16
AF = mybir.ActivationFunctionType
ALU = mybir.AluOpType
AX = mybir.AxisListType

S = 4096
D = 1024
DQK = 128
DH = 4096
NTILE = 32
NT8 = 8
MU = math.sqrt(0.5)
SD = math.sqrt(0.25 * math.pi)
DEN = SD * math.sqrt(2.0)
C_SCORE = 1.0 / (S * DEN)
K1 = math.sqrt(DQK) / DEN
CB = -MU / DEN
MASKVAL = -1.0e5
BLEN = 1152
EPS = 1e-5


def _bucket(n: int) -> int:
    if n < 16:
        return max(n, 0)
    nf = np.float32(max(n, 1))
    t = np.float32(np.log(nf / np.float32(16.0)))
    t = np.float32(t / np.float32(math.log(8.0)))
    v = np.float32(t * np.float32(16.0))
    return min(16 + int(np.int32(v)), 31)


def _emat_host() -> np.ndarray:
    e = np.zeros((32, BLEN), dtype=np.float32)
    for m in range(BLEN):
        n = m - 512
        b = _bucket(n) if n >= 0 else 0
        e[b, m] = 1.0
    return e


class Sem:
    def __init__(self, es, nc, name):
        self.h = es.enter_context(nc.semaphore(name))
        self.n = 0

    def inc(self, instr, k=1):
        instr.then_inc(self.h, k)
        self.n += k
        return self.n

    def dinc(self, instr):
        return self.inc(instr, 16)


def _w(eng, sem, val):
    if val > 0:
        eng.wait_ge(sem.h, val)


def build_program(debug=False, ntiles=8):
    nc = bass.Bass("TRN2", target_bir_lowering=False)
    dbg_kind = "ExternalOutput" if debug else "Internal"

    x_d = nc.dram_tensor("x", [S, D], F32, kind="ExternalInput")
    wqk_d = nc.dram_tensor("wqk", [D, DQK], F32, kind="ExternalInput")
    bqk_d = nc.dram_tensor("bqk", [DQK], F32, kind="ExternalInput")
    gam_d = nc.dram_tensor("gamma", [2, DQK], F32, kind="ExternalInput")
    bet_d = nc.dram_tensor("beta", [2, DQK], F32, kind="ExternalInput")
    wv_d = nc.dram_tensor("wv", [D, D], F32, kind="ExternalInput")
    bv_d = nc.dram_tensor("bv", [D], F32, kind="ExternalInput")
    relb_d = nc.dram_tensor("relb", [32], F32, kind="ExternalInput")
    emat_d = nc.dram_tensor("emat", [32, BLEN], F32, kind="ExternalInput")
    w1t_d = nc.dram_tensor("w1t", [D, DH], F32, kind="ExternalInput")
    w2t_d = nc.dram_tensor("w2t", [D, DH], F32, kind="ExternalInput")
    h1t_d = nc.dram_tensor("h1t", [DH, DH], F32, kind="ExternalInput")
    h2t_d = nc.dram_tensor("h2t", [DH, DH], F32, kind="ExternalInput")
    w3t_d = nc.dram_tensor("w3t", [DH, D], F32, kind="ExternalInput")
    w1s_d = nc.dram_tensor("w1s", [DH], F32, kind="ExternalInput")
    w2s_d = nc.dram_tensor("w2s", [DH], F32, kind="ExternalInput")
    h1s_d = nc.dram_tensor("h1s", [DH], F32, kind="ExternalInput")
    h2s_d = nc.dram_tensor("h2s", [DH], F32, kind="ExternalInput")
    w3s_d = nc.dram_tensor("w3s", [D], F32, kind="ExternalInput")
    id_d = nc.dram_tensor("ident", [128, 128], BF16, kind="ExternalInput")
    out_d = nc.dram_tensor("out", [S, D], F32, kind="ExternalOutput")

    w1b = nc.dram_tensor("w1b", [D, DH], BF16, kind="Internal")
    w2b = nc.dram_tensor("w2b", [D, DH], BF16, kind="Internal")
    h1b = nc.dram_tensor("h1b", [DH, DH], BF16, kind="Internal")
    h2b = nc.dram_tensor("h2b", [DH, DH], BF16, kind="Internal")
    w3b = nc.dram_tensor("w3b", [DH, D], BF16, kind="Internal")
    xnd = nc.dram_tensor("xnd", [NT8, 128, 4096], BF16, kind=dbg_kind)
    sd_d = nc.dram_tensor("sd", [S, D], F32, kind=dbg_kind)
    bvd = nc.dram_tensor("bvd", [BLEN], F32, kind=dbg_kind)
    toepd = nc.dram_tensor("toepd", [5, 128, 512], F32, kind=dbg_kind)
    if debug:
        dbgst_d = nc.dram_tensor("dbgst", [6, 128, 32], F32, kind="ExternalOutput")
        dbgq_d = nc.dram_tensor("dbgq", [128, S], BF16, kind="ExternalOutput")
        dbgk_d = nc.dram_tensor("dbgk", [128, S], BF16, kind="ExternalOutput")
        dbgv_d = nc.dram_tensor("dbgv", [128, 32 * 1024], BF16, kind="ExternalOutput")
        dbgw_d = nc.dram_tensor("dbgw", [128, 8 * 1024], BF16, kind="ExternalOutput")
        dbgh_d = nc.dram_tensor("dbgh", [128, 32 * 512], BF16, kind="ExternalOutput")

    es = ExitStack()
    with es:
        sems = {}

        def SM(name):
            s = Sem(es, nc, name)
            sems[name] = s
            return s

        Lc = SM("Lc")
        Lb = SM("Lb")
        Lwp = [SM("Lwp0"), SM("Lwp1")]
        Owp = [SM("Owp0"), SM("Owp1")]
        Gp = SM("Gp")
        V0 = SM("V0")
        P0 = SM("P0")
        Lx1p = [SM("Lx1a"), SM("Lx1b")]
        V1 = SM("V1")
        S1 = SM("S1")
        Lx2p = [SM("Lx2a"), SM("Lx2b")]
        V2 = SM("V2")
        S2 = SM("S2")
        P2 = SM("P2")
        G2p = [SM("G2a"), SM("G2b")]
        V3 = SM("V3")
        S3 = SM("S3")
        P3 = SM("P3")
        P3a = SM("P3a")
        G3p = [SM("G3a"), SM("G3b")]
        Lwb = [SM("Lwb0"), SM("Lwb1")]
        Lws4 = [SM(f"Lws{i}") for i in range(12)]
        Lww = SM("Lww")
        Lxn = SM("Lxn")
        Lsxp = [SM("Lsxa"), SM("Lsxb")]
        V4 = SM("V4")
        S4 = SM("S4")
        P4 = SM("P4")
        Gop = [SM("Goa"), SM("Gob")]
        Lfp = [SM(f"Lfp{i}") for i in range(8)]
        Owfp = [SM("Owf0"), SM("Owf1")]
        Owsp = [SM(f"Ows{i}") for i in range(12)]

        block = es.enter_context(nc.Block())

        def barrier():
            snap = [(s.h, s.n) for s in sems.values() if s.n > 0]
            for engf in (block.sync, block.tensor, block.scalar, block.vector,
                         block.gpsimd):
                @engf
                def _(eng, snap=snap):
                    for h, n in snap:
                        eng.wait_ge(h, n)

        epsv = es.enter_context(nc.sbuf_tensor("epsv", [128, 1], F32))
        sums = es.enter_context(nc.sbuf_tensor("sums", [128, 32], F32))
        sumsq = es.enter_context(nc.sbuf_tensor("sumsq", [128, 32], F32))
        mean_a = es.enter_context(nc.sbuf_tensor("mean_a", [128, 32], F32))
        var_a = es.enter_context(nc.sbuf_tensor("var_a", [128, 32], F32))
        std_a = es.enter_context(nc.sbuf_tensor("std_a", [128, 32], F32))
        rstd_a = es.enter_context(nc.sbuf_tensor("rstd_a", [128, 32], F32))
        sid = es.enter_context(nc.sbuf_tensor("sid", [128, 128], BF16))

        esA = ExitStack()
        with esA:
            Q = esA.enter_context(nc.sbuf_tensor("Q", [128, S], BF16))
            K = esA.enter_context(nc.sbuf_tensor("K", [128, S], BF16))
            Vt = esA.enter_context(nc.sbuf_tensor("Vt", [128, 32 * 1024], BF16))
            toep5 = esA.enter_context(nc.sbuf_tensor("toep5", [128, 5 * 512], F32))
            b31v = esA.enter_context(nc.sbuf_tensor("b31v", [128, 1], F32))
            bqkv = esA.enter_context(nc.sbuf_tensor("bqkv", [128, 1], F32))
            g0v = esA.enter_context(nc.sbuf_tensor("g0v", [128, 1], F32))
            b0v = esA.enter_context(nc.sbuf_tensor("b0v", [128, 1], F32))
            g1v = esA.enter_context(nc.sbuf_tensor("g1v", [128, 1], F32))
            b1v = esA.enter_context(nc.sbuf_tensor("b1v", [128, 1], F32))
            g0c = esA.enter_context(nc.sbuf_tensor("g0c", [128, 1], F32))
            b0c = esA.enter_context(nc.sbuf_tensor("b0c", [128, 1], F32))

            es1ab = ExitStack()
            wqk_sb = es1ab.enter_context(nc.sbuf_tensor("wqk_sb", [128, 1024], BF16))
            wvb = es1ab.enter_context(nc.sbuf_tensor("wvb", [128, 8 * 1024], BF16))
            bvbc = es1ab.enter_context(nc.sbuf_tensor("bvbc", [128, 1024], F32))
            xa = [es1ab.enter_context(nc.sbuf_tensor(f"xa{i}", [128, 1024], F32))
                  for i in range(2)]

            es0 = ExitStack()
            with es0, nc.named_scope("ph0_prep"):
                swf = [es0.enter_context(nc.sbuf_tensor(f"swf{i}", [128, 1024], F32))
                       for i in range(2)]
                relb_t = es0.enter_context(nc.sbuf_tensor("relb_t", [32, 1], F32))
                emat_t = es0.enter_context(nc.sbuf_tensor("emat_t", [32, BLEN], F32))
                bvec_sb = es0.enter_context(nc.sbuf_tensor("bvec_sb", [1, BLEN], F32))
                pb = es0.enter_context(nc.psum_tensor("pb", [1, 512], F32))

                @block.sync
                def _(sync):
                    Lc.dinc(sync.dma_start(sid[:, :], id_d[:, :]))
                    Lc.dinc(sync.dma_start(
                        bqkv[:, :], bass.AP(bqk_d, 0, [[1, 128], [1, 1]])))
                    Lc.dinc(sync.dma_start(
                        g0v[:, :], bass.AP(gam_d, 0, [[1, 128], [1, 1]])))
                    Lc.dinc(sync.dma_start(
                        b0v[:, :], bass.AP(bet_d, 0, [[1, 128], [1, 1]])))
                    Lc.dinc(sync.dma_start(
                        g1v[:, :], bass.AP(gam_d, 128, [[1, 128], [1, 1]])))
                    Lc.dinc(sync.dma_start(
                        b1v[:, :], bass.AP(bet_d, 128, [[1, 128], [1, 1]])))
                    Lc.dinc(sync.dma_start(
                        relb_t[:, :], bass.AP(relb_d, 0, [[1, 32], [1, 1]])))
                    Lc.dinc(sync.dma_start(emat_t[:, :], emat_d[:, :]))
                    Lc.dinc(sync.dma_start(
                        bvbc[:, :], bass.AP(bv_d, 0, [[0, 128], [1, 1024]])))
                    for ic in range(8):
                        Lc.dinc(sync.dma_start(
                            bass.AP(swf[0], ic * 128, [[1024, 128], [1, 128]]),
                            wqk_d[ic * 128:(ic + 1) * 128, :]))

                bp_chunks = [(0, 512), (512, 512), (1024, 128)]

                @block.vector
                def _(vector):
                    _w(vector, Lc, Lc.n)
                    vector.memset(epsv[:, :], EPS)
                    vector.tensor_scalar(g0c[:, :], g0v[:, :], C_SCORE, None, ALU.mult)
                    vector.tensor_scalar(b0c[:, :], b0v[:, :], C_SCORE, None, ALU.mult)
                    ins = vector.tensor_copy(wqk_sb[:, :], swf[0][:, 0:1024])
                    V0.inc(ins)

                for ci, (off, w) in enumerate(bp_chunks):
                    @block.tensor
                    def _(tensor, ci=ci, off=off, w=w):
                        if ci == 0:
                            _w(tensor, Lc, Lc.n)
                        else:
                            _w(tensor, V0, V0.n)
                        ins = tensor.matmul(pb[:, 0:w], relb_t[:, :],
                                            emat_t[:, off:off + w],
                                            start=True, stop=True)
                        P0.inc(ins)

                    @block.vector
                    def _(vector, ci=ci, off=off, w=w):
                        _w(vector, P0, P0.n)
                        ins = vector.tensor_scalar(bvec_sb[0:1, off:off + w],
                                                   pb[0:1, 0:w], K1, CB,
                                                   ALU.mult, ALU.add)
                        V0.inc(ins)

                @block.vector
                def _(vector):
                    vector.wait_ge(V0.h, V0.n)
                    ins = vector.memset(bvec_sb[0:1, 0:512], MASKVAL)
                    V0.inc(ins)
                v0_bvec = V0.n

                @block.gpsimd
                def _(gpsimd):
                    _w(gpsimd, V0, v0_bvec)
                    Gp.dinc(gpsimd.dma_start(
                        bass.AP(bvd, 0, [[BLEN, 1], [1, BLEN]]),
                        bvec_sb[0:1, :]))
                    gpsimd.wait_ge(Gp.h, 16)
                    Gp.dinc(gpsimd.dma_start(
                        bass.AP(toepd, 0, [[128 * 512, 5], [512, 128], [1, 512]]),
                        bass.AP(bvd, 128, [[128, 5], [-1, 128], [1, 512]])))
                    gpsimd.wait_ge(Gp.h, 32)
                    for t in range(5):
                        Gp.dinc(gpsimd.dma_start(
                            toep5[:, t * 512:(t + 1) * 512], toepd[t, :, :]))
                    Gp.dinc(gpsimd.dma_start(
                        b31v[:, :], bass.AP(bvd, BLEN - 1, [[0, 128], [1, 1]])))

                slabs = []
                for ic in range(8):
                    slabs.append(("wv", None, ic, 1024, None))

                cast_cnt = []
                out_cnt = []
                lw_cnt = []
                lb_cnt = {}
                cur_m = -1
                for k, (srcd, dstd, r, ncols, sc) in enumerate(slabs):
                    need_bcast = sc is not None and sc[0] != cur_m

                    @block.sync
                    def _(sync, k=k, srcd=srcd, r=r, ncols=ncols, sc=sc,
                          need_bcast=need_bcast):
                        if need_bcast:
                            if k >= 1:
                                _w(sync, V0, cast_cnt[k - 1])
                            Lb.dinc(sync.dma_start(
                                sbcast[:, 0:ncols],
                                bass.AP(sc[1], 0, [[0, 128], [1, ncols]])))
                        if k >= 2:
                            _w(sync, V0, cast_cnt[k - 2])
                        elif k == 0:
                            _w(sync, V0, 1)
                        src_ap = (wv_d[r * 128:(r + 1) * 128, :]
                                  if srcd == "wv" else
                                  srcd[r * 128:(r + 1) * 128, :])
                        Lwp[k % 2].dinc(sync.dma_start(swf[k % 2][:, 0:ncols],
                                                       src_ap))
                        lw_cnt.append(Lwp[k % 2].n)

                    if need_bcast:
                        cur_m = sc[0]

                    @block.vector
                    def _(vector, k=k, srcd=srcd, r=r, ncols=ncols, sc=sc):
                        _w(vector, Lwp[k % 2], lw_cnt[k])
                        if sc is not None:
                            _w(vector, Lb, 16 * (sc[0] + 1))
                        if srcd == "wv":
                            ins = vector.tensor_copy(
                                wvb[:, r * 1024:(r + 1) * 1024],
                                swf[k % 2][:, 0:1024])
                        else:
                            if k >= 2:
                                _w(vector, Owp[k % 2], out_cnt[k - 2])
                            ins = vector.tensor_tensor(
                                sbf[k % 2][:, 0:ncols], swf[k % 2][:, 0:ncols],
                                sbcast[:, 0:ncols], ALU.mult)
                        V0.inc(ins)
                        cast_cnt.append(V0.n)

                    @block.gpsimd
                    def _(gpsimd, k=k, srcd=srcd, dstd=dstd, r=r, ncols=ncols):
                        if srcd == "wv":
                            out_cnt.append(Owp[k % 2].n)
                            return
                        _w(gpsimd, V0, cast_cnt[k])
                        Owp[k % 2].dinc(gpsimd.dma_start(
                            dstd[r * 128:(r + 1) * 128, :],
                            sbf[k % 2][:, 0:ncols]))
                        out_cnt.append(Owp[k % 2].n)

            barrier()


            es1 = ExitStack()
            with es1, nc.named_scope("ph1a_stats"):
                sq = es1.enter_context(nc.sbuf_tensor("sq", [128, 1024], F32))
                v1_tile = []
                s1_tile = []

                @block.vector
                def _(vector):
                    vector.memset(sumsq[:, :], 0.0).then_inc(V1.h, 1)
                V1.n += 1
                v1_zero = V1.n
                for st in range(NTILE):
                    @block.sync
                    def _(sync, st=st):
                        if st >= 2:
                            _w(sync, V1, v1_tile[st - 2][0])
                            _w(sync, S1, v1_tile[st - 2][1])
                        Lx1p[st % 2].dinc(sync.dma_start(
                            xa[st % 2][:, :], x_d[st * 128:(st + 1) * 128, :]))

                    @block.vector
                    def _(vector, st=st):
                        _w(vector, Lx1p[st % 2], 16 * (st // 2 + 1))
                        ins = vector.reduce_sum(sums[:, st:st + 1],
                                                xa[st % 2][:, :], axis=AX.X)
                        V1.inc(ins)

                    @block.scalar
                    def _(scalar, st=st):
                        if st == 0:
                            _w(scalar, V1, v1_zero)
                        _w(scalar, Lx1p[st % 2], 16 * (st // 2 + 1))
                        ins = scalar.activation(sq[:, :], xa[st % 2][:, :],
                                                AF.Square,
                                                accum_out=sumsq[:, st:st + 1])
                        S1.inc(ins)
                    v1_tile.append((V1.n, S1.n))

                @block.vector
                def _(vector):
                    _w(vector, S1, S1.n)
                    ins = vector.tensor_scalar(mean_a[:, :], sums[:, :], 1.0 / D,
                                               None, ALU.mult)
                    V1.inc(ins)
                    vector.wait_ge(V1.h, V1.n)
                    ins = vector.tensor_tensor(sums[:, :], mean_a[:, :],
                                               mean_a[:, :], ALU.mult)
                    V1.inc(ins)
                    ins = vector.tensor_scalar(std_a[:, :], sumsq[:, :], 1.0 / D,
                                               None, ALU.mult)
                    V1.inc(ins)
                    vector.wait_ge(V1.h, V1.n)
                    ins = vector.tensor_tensor(var_a[:, :], std_a[:, :],
                                               sums[:, :], ALU.subtract)
                    V1.inc(ins)
                v1_var = V1.n

                @block.scalar
                def _(scalar):
                    _w(scalar, V1, v1_var)
                    ins = scalar.activation(std_a[:, :], var_a[:, :], AF.Sqrt,
                                            bias=epsv[:, :])
                    S1.inc(ins)

                @block.vector
                def _(vector):
                    _w(vector, S1, S1.n)
                    ins = vector.reciprocal(rstd_a[:, :], std_a[:, :])
                    V1.inc(ins)

            barrier()

            es1b = ExitStack()
            with es1b, nc.named_scope("ph1b_qkv"):
                xn_tm = [es1b.enter_context(
                    nc.sbuf_tensor(f"xn_tm{i}", [128, 1024], BF16)) for i in range(2)]
                xnf = [es1b.enter_context(nc.sbuf_tensor(f"xnf{i}", [128, 4096], BF16))
                       for i in range(2)]
                qkt = es1b.enter_context(nc.sbuf_tensor("qkt", [128, 512], F32))
                vtmp = [es1b.enter_context(
                    nc.sbuf_tensor(f"vtmp{i}", [128, 512], F32)) for i in range(8)]
                ptp = [es1b.enter_context(
                    nc.psum_tensor(f"ptp{i}", [128, 512], BF16)) for i in range(2)]
                pqk = es1b.enter_context(nc.psum_tensor("pqk", [128, 512], F32))
                pv = [es1b.enter_context(
                    nc.psum_tensor(f"pv{i}", [128, 512], F32)) for i in range(2)]

                ln_cnt = []
                tr_cnt = []
                ev_cnt = []
                vadd_cnt = []
                vsil_cnt = []
                pv_cnt = []
                qks_cnt = []
                qkaff_cnt = []
                xst_cnt = []

                for t8 in range(NT8):
                    for sub in range(4):
                        g = t8 * 4 + sub

                        @block.sync
                        def _(sync, g=g):
                            if g >= 2:
                                _w(sync, V2, ln_cnt[g - 2])
                            Lx2p[g % 2].dinc(sync.dma_start(
                                xa[g % 2][:, :], x_d[g * 128:(g + 1) * 128, :]))

                        @block.vector
                        def _(vector, g=g, sub=sub, t8=t8):
                            _w(vector, Lx2p[g % 2], 16 * (g // 2 + 1))
                            if g >= 2:
                                _w(vector, P2, tr_cnt[g - 2])
                            ins = vector.tensor_scalar(
                                xn_tm[g % 2][:, :], xa[g % 2][:, :],
                                mean_a[:, g:g + 1], rstd_a[:, g:g + 1],
                                ALU.subtract, ALU.mult)
                            V2.inc(ins)
                            ln_cnt.append(V2.n)

                        @block.tensor
                        def _(tensor, g=g, sub=sub, t8=t8):
                            _w(tensor, V2, ln_cnt[g])
                            if g >= 1:
                                _w(tensor, V2, ev_cnt[g - 1])
                            for ic in range(8):
                                ins = tensor.transpose(
                                    ptp[ic // 4][:, (ic % 4) * 128:(ic % 4 + 1) * 128],
                                    xn_tm[g % 2][:, ic * 128:(ic + 1) * 128],
                                    sid[:, :])
                                if ic == 3 or ic == 7:
                                    P2.inc(ins)
                            tr_cnt.append(P2.n)

                        @block.vector
                        def _(vector, g=g, sub=sub, t8=t8):
                            if t8 >= 2 and sub == 0:
                                _w(vector, xst_cnt[t8 - 2][0], xst_cnt[t8 - 2][1])
                                _w(vector, P2, pv_cnt[(t8 - 2) * 8 + 7])
                            for h in range(2):
                                _w(vector, P2, tr_cnt[g] - 1 + h)
                                ins = vector.tensor_copy(
                                    bass.AP(xnf[t8 % 2], sub * 128 + h * 4 * 512,
                                            [[4096, 128], [512, 4], [1, 128]]),
                                    ptp[h][:, :])
                                V2.inc(ins)
                            ev_cnt.append(V2.n)

                    @block.tensor
                    def _(tensor, t8=t8):
                        _w(tensor, V2, ev_cnt[t8 * 4 + 3])
                        if t8 >= 1:
                            _w(tensor, S2, qks_cnt[t8 - 1])
                        for ic in range(8):
                            ins = tensor.matmul(pqk[:, :],
                                                wqk_sb[:, ic * 128:(ic + 1) * 128],
                                                xnf[t8 % 2][:, ic * 512:(ic + 1) * 512],
                                                start=(ic == 0), stop=(ic == 7))
                        P2.inc(ins)
                    p2_qk = P2.n

                    @block.scalar
                    def _(scalar, t8=t8, p2_qk=p2_qk):
                        _w(scalar, P2, p2_qk)
                        if t8 >= 1:
                            _w(scalar, V2, qkaff_cnt[t8 - 1])
                        ins = scalar.activation(qkt[:, :], pqk[:, :], AF.Silu,
                                                bias=bqkv[:, :])
                        S2.inc(ins)
                        qks_cnt.append(S2.n)

                    @block.vector
                    def _(vector, t8=t8):
                        _w(vector, S2, qks_cnt[t8])
                        vector.tensor_scalar(Q[:, t8 * 512:(t8 + 1) * 512],
                                             qkt[:, :], g0c[:, :], b0c[:, :],
                                             ALU.mult, ALU.add)
                        ins = vector.tensor_scalar(K[:, t8 * 512:(t8 + 1) * 512],
                                                   qkt[:, :], g1v[:, :], b1v[:, :],
                                                   ALU.mult, ALU.add)
                        V2.inc(ins)
                        qkaff_cnt.append(V2.n)

                    for sub in range(4):
                        for oc2 in range(2):
                            gi = t8 * 8 + sub * 2 + oc2

                            @block.tensor
                            def _(tensor, t8=t8, sub=sub, oc2=oc2, gi=gi):
                                if gi >= 2:
                                    _w(tensor, V2, vadd_cnt[gi - 2])
                                for ic in range(8):
                                    ins = tensor.matmul(
                                        pv[gi % 2][:, :],
                                        xnf[t8 % 2][:, ic * 512 + sub * 128:
                                            ic * 512 + (sub + 1) * 128],
                                        wvb[:, ic * 1024 + oc2 * 512:
                                            ic * 1024 + (oc2 + 1) * 512],
                                        start=(ic == 0), stop=(ic == 7))
                                P2.inc(ins)
                                pv_cnt.append(P2.n)

                            @block.vector
                            def _(vector, gi=gi, oc2=oc2):
                                _w(vector, P2, pv_cnt[gi])
                                if gi >= 8:
                                    _w(vector, S2, vsil_cnt[gi - 8])
                                ins = vector.tensor_tensor(
                                    vtmp[gi % 8][:, :], pv[gi % 2][:, :],
                                    bvbc[:, oc2 * 512:(oc2 + 1) * 512], ALU.add)
                                V2.inc(ins)
                                vadd_cnt.append(V2.n)

                    @block.scalar
                    def _(scalar, t8=t8):
                        for sub in range(4):
                            for oc2 in range(2):
                                gi = t8 * 8 + sub * 2 + oc2
                                _w(scalar, V2, vadd_cnt[gi])
                                jc = t8 * 4 + sub
                                ins = scalar.activation(
                                    Vt[:, jc * 1024 + oc2 * 512:
                                       jc * 1024 + (oc2 + 1) * 512],
                                    vtmp[gi % 8][:, :], AF.Silu)
                                S2.inc(ins)
                                vsil_cnt.append(S2.n)

                    @block.gpsimd
                    def _(gpsimd, t8=t8):
                        _w(gpsimd, S2, vsil_cnt[t8 * 8 + 7])
                        G2p[t8 % 2].dinc(gpsimd.dma_start(
                            xnd[t8, :, :], xnf[t8 % 2][:, :]))
                        xst_cnt.append((G2p[t8 % 2], G2p[t8 % 2].n))

            es1ab.close()
            barrier()

            if debug:
                @block.gpsimd
                def _(gpsimd):
                    for h, n in [(sm.h, sm.n) for sm in sems.values() if sm.n > 0]:
                        gpsimd.wait_ge(h, n)
                    for t, src in [(0, sums), (1, sumsq), (2, mean_a),
                                   (3, var_a), (4, std_a), (5, rstd_a)]:
                        Gp.dinc(gpsimd.dma_start(dbgst_d[t, :, :], src[:, :]))
                    Gp.dinc(gpsimd.dma_start(dbgq_d[:, :], Q[:, :]))
                    Gp.dinc(gpsimd.dma_start(dbgk_d[:, :], K[:, :]))
                    Gp.dinc(gpsimd.dma_start(dbgv_d[:, :], Vt[:, :]))
                    Gp.dinc(gpsimd.dma_start(dbgw_d[:, :], wvb[:, :]))

            es2 = ExitStack()
            with es2, nc.named_scope("ph2_attn"):
                attst = [es2.enter_context(
                    nc.sbuf_tensor(f"attst{i}", [128, 32 * 512], BF16))
                    for i in range(2)]
                t1w = es2.enter_context(nc.sbuf_tensor("t1w", [128, 2048], F32))
                e4 = es2.enter_context(nc.sbuf_tensor("e4", [128, 2048], F32))
                sstg = [es2.enter_context(
                    nc.sbuf_tensor(f"sstg{i}", [128, 1024], F32)) for i in range(2)]
                psc = [es2.enter_context(
                    nc.psum_tensor(f"psc{i}", [128, 512], F32)) for i in range(2)]
                pav = [es2.enter_context(
                    nc.psum_tensor(f"pav{i}", [128, 512], F32)) for i in range(4)]

                NJ = [4 * i + 4 for i in range(8)]
                sc_idx = 0
                psc_free = []
                t1w_free = (None, 0)
                e4_free = (None, 0)
                aff_cnt = {}
                aff_blk = {}
                erf_cnt = {}
                av_done = []
                evac_cnt = {}
                store_cnt = {}
                pav_free = [(None, 0)] * 4
                sstg_free = [(None, 0)] * 2

                def emit_scores(I):
                    nonlocal sc_idx, t1w_free, e4_free
                    nj = NJ[I]
                    for Jb in range(nj // 4):
                        for j4 in range(4):
                            Jc = Jb * 4 + j4
                            gidx = sc_idx
                            sc_idx += 1
                            off = 512 * I - 128 * Jc
                            near = off <= 128
                            par = gidx % 2

                            @block.tensor
                            def _(tensor, I=I, Jc=Jc, gidx=gidx, par=par):
                                if gidx >= 2:
                                    h, n = psc_free[gidx - 2]
                                    _w(tensor, h, n)
                                ins = tensor.matmul(
                                    psc[par][:, :],
                                    K[:, Jc * 128:(Jc + 1) * 128],
                                    Q[:, I * 512:(I + 1) * 512],
                                    start=True, stop=True)
                                P3.inc(ins)
                            p3_here = P3.n

                            @block.vector
                            def _(vector, I=I, Jc=Jc, j4=j4, par=par,
                                  near=near, p3_here=p3_here):
                                _w(vector, P3, p3_here)
                                if j4 == 0:
                                    h, n = t1w_free
                                    if h is not None:
                                        _w(vector, h, n)
                                dst = t1w[:, j4 * 512:(j4 + 1) * 512]
                                if near:
                                    tidx = 4 * I - Jc + 3
                                    ins = vector.tensor_tensor(
                                        dst, psc[par][:, :],
                                        toep5[:, tidx * 512:(tidx + 1) * 512],
                                        ALU.add)
                                else:
                                    ins = vector.tensor_scalar(
                                        dst, psc[par][:, :], b31v[:, :], None,
                                        ALU.add)
                                V3.inc(ins)
                            psc_free.append((V3, V3.n))

                        @block.scalar
                        def _(scalar, I=I, Jb=Jb):
                            _w(scalar, V3, V3.n)
                            h, n = e4_free
                            if h is not None:
                                _w(scalar, h, n)
                            ins = scalar.activation(e4[:, :], t1w[:, :], AF.Erf)
                            S3.inc(ins)
                        t1w_free = (S3, S3.n)

                        @block.vector
                        def _(vector, I=I, Jb=Jb):
                            _w(vector, S3, S3.n)
                            if I >= 2 and Jb == 0:
                                _w(vector, P3a, 8 * (I - 1))
                            ins = vector.tensor_scalar(
                                attst[I % 2][:, Jb * 2048:(Jb + 1) * 2048],
                                e4[:, :], 0.5, 0.5, ALU.mult, ALU.add)
                            V3.inc(ins)
                        e4_free = (V3, V3.n)
                        aff_blk[(I, Jb)] = V3.n
                    aff_cnt[I] = V3.n

                def emit_av(I):
                    nj = NJ[I]
                    for q in range(4):
                        for dv in range(2):
                            gidx = q * 2 + dv
                            bank = gidx % 4

                            @block.tensor
                            def _(tensor, I=I, q=q, dv=dv, gidx=gidx, bank=bank,
                                  nj=nj):
                                h, n = pav_free[bank]
                                if h is not None:
                                    _w(tensor, h, n)
                                for Jc in range(nj):
                                    if gidx == 0 and Jc % 4 == 0:
                                        _w(tensor, V3, aff_blk[(I, Jc // 4)])
                                    ins = tensor.matmul(
                                        pav[bank][:, :],
                                        attst[I % 2][:, Jc * 512 + q * 128:
                                                     Jc * 512 + (q + 1) * 128],
                                        Vt[:, Jc * 1024 + dv * 512:
                                           Jc * 1024 + (dv + 1) * 512],
                                        start=(Jc == 0), stop=(Jc == nj - 1))
                                P3a.inc(ins)
                            p3a_here = P3a.n

                            @block.vector
                            def _(vector, I=I, q=q, dv=dv, bank=bank,
                                  p3a_here=p3a_here):
                                _w(vector, P3a, p3a_here)
                                if dv == 0:
                                    h, n = sstg_free[q % 2]
                                    if h is not None:
                                        _w(vector, h, n)
                                ins = vector.tensor_copy(
                                    sstg[q % 2][:, dv * 512:(dv + 1) * 512],
                                    pav[bank][:, :])
                                V3.inc(ins)
                            pav_free[bank] = (V3, V3.n)
                            evac_cnt[(I, gidx)] = V3.n

                        @block.gpsimd
                        def _(gpsimd, I=I, q=q):
                            _w(gpsimd, V3, evac_cnt[(I, q * 2 + 1)])
                            G3p[q % 2].dinc(gpsimd.dma_start(
                                sd_d[I * 512 + q * 128:I * 512 + (q + 1) * 128, :],
                                sstg[q % 2][:, :]))
                        sstg_free[q % 2] = (G3p[q % 2], G3p[q % 2].n)

                for I in range(8):
                    emit_scores(I)
                    if I >= 1:
                        emit_av(I - 1)
                emit_av(7)

            barrier()

        esB = ExitStack()
        with esB, nc.named_scope("ph3_ffn"):
            xnf3 = esB.enter_context(nc.sbuf_tensor("xnf3", [128, 4096], BF16))
            hA = esB.enter_context(nc.sbuf_tensor("hA", [128, 32 * 512], BF16))
            hB = esB.enter_context(nc.sbuf_tensor("hB", [128, 32 * 512], BF16))
            wb1 = [esB.enter_context(nc.sbuf_tensor(f"wb1_{i}", [128, 4096], BF16))
                   for i in range(2)]
            wb2 = [esB.enter_context(nc.sbuf_tensor(f"wb2_{i}", [128, 4096], BF16))
                   for i in range(2)]
            wsm = [esB.enter_context(nc.sbuf_tensor(f"wsm{i}", [128, 1024], BF16))
                   for i in range(12)]
            tsil = [esB.enter_context(nc.sbuf_tensor(f"tsil{i}", [128, 512], F32))
                    for i in range(4)]
            gts4 = esB.enter_context(nc.sbuf_tensor("gts4", [128, 4096], F32))
            s_t = [esB.enter_context(nc.sbuf_tensor(f"s_t{i}", [128, 1024], F32))
                   for i in range(2)]
            cmb = esB.enter_context(nc.sbuf_tensor("cmb", [128, 1024], F32))
            x_t = [esB.enter_context(nc.sbuf_tensor(f"x_t{i}", [128, 1024], F32))
                   for i in range(2)]
            swf3 = [esB.enter_context(nc.sbuf_tensor(f"swf3_{i}", [128, 1024], F32))
                    for i in range(5)]
            sbc1p = [esB.enter_context(nc.sbuf_tensor(f"sbc1_{i}", [128, 1024], F32))
                     for i in range(2)]
            sbc2p = [esB.enter_context(nc.sbuf_tensor(f"sbc2_{i}", [128, 512], F32))
                     for i in range(2)]
            ps8 = [esB.enter_context(nc.psum_tensor(f"ps8_{i}", [128, 512], F32))
                   for i in range(8)]


            bank_free = [(None, 0)] * 8
            wsm_free = [(None, 0)] * 12
            wb_free = [(None, 0), (None, 0)]
            xnf3_free = (None, 0)
            hA_w_ok = (None, 0)
            hB_w_ok = (None, 0)
            tsil_free = [(None, 0)] * 4
            gts_free = (None, 0)
            sx_free = [(None, 0)] * 2
            st1_done = (None, 0)

            st0_cast = []
            sbc_war2 = [(None, 0), (None, 0)]
            pre = {}
            v4_mult = {}
            s23_mm = {}
            for tk in range(ntiles):
                s4b = S4.n
                p_sil1 = lambda g, o4, s4b=s4b: s4b + g * 4 + o4 + 1
                p_gel = lambda st, g, o8, s4b=s4b: s4b + 32 + st * 32 + g * 8 + o8 + 1
                p_sil4 = lambda q, od, s4b=s4b: s4b + 96 + q * 2 + od + 1
                if tk == 1:
                    snap0 = ([(h.h, h.n) for h in Owfp if h.n] +
                             [(h.h, h.n) for h in Owsp if h.n])
                    for engf in (block.sync, block.gpsimd):
                        @engf
                        def _(eng, snap0=snap0):
                            for hh, nn in snap0:
                                eng.wait_ge(hh, nn)
                if tk in pre:
                    lxn_here = pre[tk]["xn"]
                else:
                    @block.sync
                    def _(sync, tk=tk):
                        h, n = xnf3_free
                        if h is not None:
                            _w(sync, h, n)
                        Lxn.dinc(sync.dma_start(xnf3[:, :], xnd[tk, :, :]))
                    lxn_here = Lxn.n

                @block.gpsimd
                def _(gpsimd, tk=tk):
                    for q in range(2):
                        h, n = sx_free[q % 2]
                        if h is not None:
                            _w(gpsimd, h, n)
                        Lsxp[q % 2].dinc(gpsimd.dma_start(
                            s_t[q % 2][:, :],
                            sd_d[tk * 512 + q * 128:tk * 512 + (q + 1) * 128, :]))
                        Lsxp[q % 2].dinc(gpsimd.dma_start(
                            x_t[q % 2][:, :],
                            x_d[tk * 512 + q * 128:tk * 512 + (q + 1) * 128, :]))
                sx_here = [(Lsxp[0], Lsxp[0].n), (Lsxp[1], Lsxp[1].n)]

                s4b1 = S4.n
                st1_mult = {}
                st1_sil = {}
                def st1_deliver(g):
                    if tk in pre and g in pre[tk]:
                        return pre[tk][g]
                    if tk == 0:
                        ch = [(w1t_d, w1b, wb1, ic) for ic in range(8)] + \
                             [(w2t_d, w2b, wb2, ic) for ic in range(8)]
                        base = len(st0_cast)
                        lf_cnt = []

                        @block.sync
                        def _(sync, g=g, ch=ch, base=base, lf=lf_cnt,
                              sbw=sbc_war2[g % 2], v4_at=V4.n):
                            if sbw[0] is not None:
                                _w(sync, sbw[0], sbw[1])
                            Lb.dinc(sync.dma_start(
                                sbc1p[g % 2][:, 0:512],
                                bass.AP(w1s_d, g * 512, [[0, 128], [1, 512]])))
                            Lb.dinc(sync.dma_start(
                                sbc2p[g % 2][:, 0:512],
                                bass.AP(w2s_d, g * 512, [[0, 128], [1, 512]])))
                            for j, (srcd, dstd, wbx, ic) in enumerate(ch):
                                idx = base + j
                                if idx >= 5:
                                    if idx - 5 < base:
                                        _w(sync, V4, st0_cast[idx - 5])
                                    else:
                                        _w(sync, V4, v4_at + (j - 5) + 1)
                                Lfp[idx % 5].dinc(sync.dma_start(
                                    swf3[idx % 5][:, 0:512],
                                    srcd[ic * 128:(ic + 1) * 128,
                                         g * 512:(g + 1) * 512]))
                                lf.append(Lfp[idx % 5].n)
                        lb_here = Lb.n

                        @block.vector
                        def _(vector, g=g, ch=ch, base=base, lf=lf_cnt,
                              lb_here=lb_here):
                            h, n = wb_free[g % 2]
                            if h is not None:
                                _w(vector, h, n)
                            if g >= 2:
                                _w(vector, Owfp[g % 2], Owfp[g % 2].n)
                            _w(vector, Lb, lb_here)
                            for j, (srcd, dstd, wbx, ic) in enumerate(ch):
                                idx = base + j
                                _w(vector, Lfp[idx % 5], lf[j])
                                sb = (sbc1p if wbx is wb1 else sbc2p)[g % 2]
                                ins = vector.tensor_tensor(
                                    wbx[g % 2][:, ic * 512:(ic + 1) * 512],
                                    swf3[idx % 5][:, 0:512], sb[:, 0:512],
                                    ALU.mult)
                                V4.inc(ins)
                                st0_cast.append(V4.n)
                        sbc_war2[g % 2] = (V4, V4.n)

                        @block.gpsimd
                        def _(gpsimd, g=g, ch=ch, base=base):
                            for j, (srcd, dstd, wbx, ic) in enumerate(ch):
                                _w(gpsimd, V4, st0_cast[base + j])
                                Owfp[g % 2].dinc(gpsimd.dma_start(
                                    dstd[ic * 128:(ic + 1) * 128,
                                         g * 512:(g + 1) * 512],
                                    wbx[g % 2][:, ic * 512:(ic + 1) * 512]))
                        return (V4, st0_cast[-1])
                    else:
                        @block.sync
                        def _(sync, tk=tk, g=g):
                            h, n = wb_free[g % 2]
                            if h is not None:
                                _w(sync, h, n)
                            for ic in range(8):
                                Lwb[g % 2].dinc(sync.dma_start(
                                    wb1[g % 2][:, ic * 512:(ic + 1) * 512],
                                    w1b[ic * 128:(ic + 1) * 128,
                                        g * 512:(g + 1) * 512]))
                                Lwb[g % 2].dinc(sync.dma_start(
                                    wb2[g % 2][:, ic * 512:(ic + 1) * 512],
                                    w2b[ic * 128:(ic + 1) * 128,
                                        g * 512:(g + 1) * 512]))
                        return (Lwb[g % 2], Lwb[g % 2].n)

                def st1_consume(g, lw3_here):
                    @block.tensor
                    def _(tensor, tk=tk, g=g, lw3_here=lw3_here,
                          lxn_here=lxn_here):
                        _w(tensor, lw3_here[0], lw3_here[1])
                        if g == 0:
                            _w(tensor, Lxn, lxn_here)
                        for o4 in range(4):
                            h, n = bank_free[o4]
                            if h is not None:
                                _w(tensor, h, n)
                            h, n = bank_free[4 + o4]
                            if h is not None:
                                _w(tensor, h, n)
                        for t in range(14):
                            for o4 in range(4):
                                ic = t - 2 * o4
                                if ic < 0 or ic > 7:
                                    continue
                                ins = tensor.matmul(
                                    ps8[o4][:, :],
                                    wb1[g % 2][:, ic * 512 + o4 * 128:
                                               ic * 512 + (o4 + 1) * 128],
                                    xnf3[:, ic * 512:(ic + 1) * 512],
                                    start=(ic == 0), stop=(ic == 7))
                                if ic == 7:
                                    P4.inc(ins)
                                    st1_mult[(g, o4, "y1")] = P4.n
                                ins = tensor.matmul(
                                    ps8[4 + o4][:, :],
                                    wb2[g % 2][:, ic * 512 + o4 * 128:
                                               ic * 512 + (o4 + 1) * 128],
                                    xnf3[:, ic * 512:(ic + 1) * 512],
                                    start=(ic == 0), stop=(ic == 7))
                                if ic == 7:
                                    P4.inc(ins)
                                    st1_mult[(g, o4, "y2")] = P4.n
                    wb_free[g % 2] = (P4, P4.n)
                    if g == 7:
                        nonlocal_box[0] = (P4, P4.n)

                    @block.vector
                    def _(vector, g=g, tk=tk):
                        for o4 in range(4):
                            oc = g * 4 + o4
                            _w(vector, S4, p_sil1(g, o4))
                            _w(vector, P4, st1_mult[(g, o4, "y2")])
                            if g == 0 and o4 == 0:
                                h, n = hA_w_ok
                                if h is not None:
                                    _w(vector, h, n)
                            ins = vector.tensor_tensor(
                                hA[:, oc * 512:(oc + 1) * 512],
                                tsil[o4][:, :], ps8[4 + o4][:, :], ALU.mult)
                            V4.inc(ins)
                            bank_free[4 + o4] = (V4, V4.n)
                            v4_mult[(tk, g, o4)] = V4.n
                    for o4 in range(4):
                        bank_free[o4] = (S4, p_sil1(g, o4))

                nonlocal_box = [None]
                lw_pend = st1_deliver(0)
                for g in range(8):
                    nxt = st1_deliver(g + 1) if g + 1 < 8 else None
                    st1_consume(g, lw_pend)
                    lw_pend = nxt
                if nonlocal_box[0] is not None:
                    xnf3_free = nonlocal_box[0]
                st1_done = (V4, V4.n)

                if tk + 1 < ntiles:
                    pre[tk + 1] = {}

                    @block.gpsimd
                    def _(gpsimd, tk=tk):
                        h, n = xnf3_free
                        if h is not None:
                            _w(gpsimd, h, n)
                        Lxn.dinc(gpsimd.dma_start(xnf3[:, :], xnd[tk + 1, :, :]))
                    pre[tk + 1]["xn"] = Lxn.n

                    for g in range(2):
                        @block.gpsimd
                        def _(gpsimd, tk=tk, g=g):
                            if tk == 0:
                                for hh in Owfp:
                                    if hh.n:
                                        gpsimd.wait_ge(hh.h, hh.n)
                            h, n = wb_free[g % 2]
                            if h is not None:
                                _w(gpsimd, h, n)
                            for ic in range(8):
                                Lwb[g % 2].dinc(gpsimd.dma_start(
                                    wb1[g % 2][:, ic * 512:(ic + 1) * 512],
                                    w1b[ic * 128:(ic + 1) * 128,
                                        g * 512:(g + 1) * 512]))
                                Lwb[g % 2].dinc(gpsimd.dma_start(
                                    wb2[g % 2][:, ic * 512:(ic + 1) * 512],
                                    w2b[ic * 128:(ic + 1) * 128,
                                        g * 512:(g + 1) * 512]))
                        pre[tk + 1][g] = (Lwb[g % 2], Lwb[g % 2].n)

                s4b32 = s4b1 + 32
                p_gel = lambda st, g, o8, b=None: s4b32 + st * 32 + g * 8 + o8 + 1
                for stage, (wsrc, wsrcf, scv, hin, hout) in enumerate(
                        [(h1b, h1t_d, h1s_d, hA, hB),
                         (h2b, h2t_d, h2s_d, hB, hA)]):
                    for g in range(4):
                        blk_done = {}
                        if tk == 0:
                            @block.sync
                            def _(sync, scv=scv, g=g, sbw=sbc_war2[g % 2]):
                                if sbw[0] is not None:
                                    _w(sync, sbw[0], sbw[1])
                                Lb.dinc(sync.dma_start(
                                    sbc1p[g % 2][:, :],
                                    bass.AP(scv, g * 1024, [[0, 128], [1, 1024]])))
                            lb_here = Lb.n
                        for b in range(9):
                            tlo, thi = 4 * b, min(4 * b + 3, 34)
                            if b <= 7:
                                if tk == 0:
                                    base = len(st0_cast)
                                    lf_cnt = []

                                    @block.sync
                                    def _(sync, wsrcf=wsrcf, g=g, b=b, base=base,
                                          lf=lf_cnt):
                                        for j, ic in enumerate(range(4 * b, 4 * b + 4)):
                                            idx = base + j
                                            if idx >= 4:
                                                _w(sync, V4, st0_cast[idx - 4])
                                            Lfp[idx % 5].dinc(sync.dma_start(
                                                swf3[idx % 5][:, :],
                                                wsrcf[ic * 128:(ic + 1) * 128,
                                                      g * 1024:(g + 1) * 1024]))
                                            lf.append(Lfp[idx % 5].n)

                                    @block.vector
                                    def _(vector, g=g, b=b, base=base, lf=lf_cnt,
                                          lb_here=lb_here):
                                        _w(vector, Lb, lb_here)
                                        for j, ic in enumerate(range(4 * b, 4 * b + 4)):
                                            idx = base + j
                                            _w(vector, Lfp[idx % 5], lf[j])
                                            h, n = wsm_free[ic % 12]
                                            if h is not None:
                                                _w(vector, h, n)
                                            if Owsp[ic % 12].n:
                                                _w(vector, Owsp[ic % 12],
                                                   Owsp[ic % 12].n)
                                            ins = vector.tensor_tensor(
                                                wsm[ic % 12][:, :],
                                                swf3[idx % 5][:, :],
                                                sbc1p[g % 2][:, :], ALU.mult)
                                            V4.inc(ins)
                                            st0_cast.append(V4.n)
                                    sbc_war2[g % 2] = (V4, V4.n)

                                    @block.gpsimd
                                    def _(gpsimd, wsrc=wsrc, g=g, b=b, base=base):
                                        for j, ic in enumerate(range(4 * b, 4 * b + 4)):
                                            _w(gpsimd, V4, st0_cast[base + j])
                                            Owsp[ic % 12].dinc(gpsimd.dma_start(
                                                wsrc[ic * 128:(ic + 1) * 128,
                                                     g * 1024:(g + 1) * 1024],
                                                wsm[ic % 12][:, :]))
                                    lw3_blk = [(V4, st0_cast[base + j])
                                               for j in range(4)]
                                else:
                                    @block.sync
                                    def _(sync, wsrc=wsrc, g=g, b=b):
                                        for ic in range(4 * b, 4 * b + 4):
                                            h, n = wsm_free[ic % 12]
                                            if h is not None:
                                                _w(sync, h, n)
                                            Lws4[ic % 12].dinc(sync.dma_start(
                                                wsm[ic % 12][:, :],
                                                wsrc[ic * 128:(ic + 1) * 128,
                                                     g * 1024:(g + 1) * 1024]))
                                    lw3_blk = [(Lws4[(4 * b + j) % 12],
                                                Lws4[(4 * b + j) % 12].n)
                                               for j in range(4)]

                            @block.tensor
                            def _(tensor, stage=stage, g=g, b=b, hin=hin,
                                  tlo=tlo, thi=thi,
                                  lw3_blk=(lw3_blk if b <= 7 else None)):
                                if lw3_blk is not None:
                                    for h, n in lw3_blk:
                                        _w(tensor, h, n)
                                if b == 0 and g == 0:
                                    if stage == 0:
                                        h, n = st1_done
                                        _w(tensor, h, n)
                                if b == 0:
                                    for o8 in range(8):
                                        h, n = bank_free[o8]
                                        if h is not None:
                                            _w(tensor, h, n)
                                for t in range(tlo, thi + 1):
                                    for o8 in range(8):
                                        c = t - (o8 // 2)
                                        if c < 0 or c > 31:
                                            continue
                                        ins = tensor.matmul(
                                            ps8[o8][:, :],
                                            wsm[c % 12][:, o8 * 128:(o8 + 1) * 128],
                                            hin[:, c * 512:(c + 1) * 512],
                                            start=(c == 0), stop=(c == 31))
                                        if c == 31:
                                            P4.inc(ins)
                                            blk_done[o8] = P4.n
                                if b < 8:
                                    P4.inc(ins)
                            for t in range(tlo, thi + 1):
                                c = t - 3
                                if 0 <= c <= 31:
                                    wsm_free[c % 12] = (P4, P4.n)
                        for o8 in range(8):
                            s23_mm[(tk, stage, g, o8)] = blk_done[o8]
                            bank_free[o8] = (S4, p_gel(stage, g, o8))
                st3_done = (S4, 0)


                st4_mm = {}
                st4_sil = {}
                comb_cnt = {}
                if tk == 0:
                    @block.sync
                    def _(sync, sbw=sbc_war2[0]):
                        if sbw[0] is not None:
                            _w(sync, sbw[0], sbw[1])
                        Lb.dinc(sync.dma_start(
                            sbc1p[0][:, :],
                            bass.AP(w3s_d, 0, [[0, 128], [1, 1024]])))
                    lb_here = Lb.n
                for b4 in range(9):
                    tlo, thi = 4 * b4, min(4 * b4 + 3, 34)
                    if b4 <= 7:
                        if tk == 0:
                            base = len(st0_cast)
                            lf_cnt = []

                            @block.sync
                            def _(sync, b4=b4, base=base, lf=lf_cnt):
                                for j, ic in enumerate(range(4 * b4, 4 * b4 + 4)):
                                    idx = base + j
                                    if idx >= 5:
                                        _w(sync, V4, st0_cast[idx - 5])
                                    Lfp[idx % 5].dinc(sync.dma_start(
                                        swf3[idx % 5][:, :],
                                        w3t_d[ic * 128:(ic + 1) * 128, :]))
                                    lf.append(Lfp[idx % 5].n)

                            @block.vector
                            def _(vector, b4=b4, base=base, lf=lf_cnt,
                                  lb_here=lb_here):
                                _w(vector, Lb, lb_here)
                                for j, ic in enumerate(range(4 * b4, 4 * b4 + 4)):
                                    idx = base + j
                                    _w(vector, Lfp[idx % 5], lf[j])
                                    h, n = wsm_free[ic % 12]
                                    if h is not None:
                                        _w(vector, h, n)
                                    if Owsp[ic % 12].n:
                                        _w(vector, Owsp[ic % 12], Owsp[ic % 12].n)
                                    ins = vector.tensor_tensor(
                                        wsm[ic % 12][:, :],
                                        swf3[idx % 5][:, :], sbc1p[0][:, :],
                                        ALU.mult)
                                    V4.inc(ins)
                                    st0_cast.append(V4.n)
                            sbc_war2[0] = (V4, V4.n)

                            @block.gpsimd
                            def _(gpsimd, b4=b4, base=base):
                                for j, ic in enumerate(range(4 * b4, 4 * b4 + 4)):
                                    _w(gpsimd, V4, st0_cast[base + j])
                                    Owsp[ic % 12].dinc(gpsimd.dma_start(
                                        w3b[ic * 128:(ic + 1) * 128, :],
                                        wsm[ic % 12][:, :]))
                            lw4_blk = [(V4, st0_cast[base + j]) for j in range(4)]
                        else:
                            @block.sync
                            def _(sync, tk=tk, b4=b4):
                                for ic in range(4 * b4, 4 * b4 + 4):
                                    h, n = wsm_free[ic % 12]
                                    if h is not None:
                                        _w(sync, h, n)
                                    Lws4[ic % 12].dinc(sync.dma_start(
                                        wsm[ic % 12][:, :],
                                        w3b[ic * 128:(ic + 1) * 128, :]))
                            lw4_blk = [(Lws4[(4 * b4 + j) % 12],
                                        Lws4[(4 * b4 + j) % 12].n) for j in range(4)]

                    @block.tensor
                    def _(tensor, tk=tk, b4=b4, tlo=tlo, thi=thi,
                          lw4_blk=(lw4_blk if b4 <= 7 else None),
                          st3_done=st3_done):
                        if lw4_blk is not None:
                            for h, n in lw4_blk:
                                _w(tensor, h, n)
                        if b4 == 0:
                            _w(tensor, st3_done[0], st3_done[1])
                            for bank in range(8):
                                h, n = bank_free[bank]
                                if h is not None:
                                    _w(tensor, h, n)
                        for t in range(tlo, thi + 1):
                            for q in range(4):
                                for od in range(2):
                                    bank = q * 2 + od
                                    ic = t - (bank // 2)
                                    if ic < 0 or ic > 31:
                                        continue
                                    ins = tensor.matmul(
                                        ps8[bank][:, :],
                                        hA[:, ic * 512 + q * 128:
                                           ic * 512 + (q + 1) * 128],
                                        wsm[ic % 12][:, od * 512:(od + 1) * 512],
                                        start=(ic == 0), stop=(ic == 31))
                                    if ic == 31:
                                        P4.inc(ins)
                                        st4_mm[bank] = P4.n
                        if b4 < 8:
                            P4.inc(ins)
                    for t in range(tlo, thi + 1):
                        c = t - 3
                        if 0 <= c <= 31:
                            wsm_free[c % 12] = (P4, P4.n)

                s4q = s4b1 + 96
                gts_free0 = gts_free
                for q in range(4):
                    for od in range(2):
                        st4_sil[(q, od)] = s4q + q * 2 + od + 1
                        bank_free[q * 2 + od] = (S4, s4q + q * 2 + od + 1)

                for q in range(4):
                    if q >= 2:
                        @block.gpsimd
                        def _(gpsimd, tk=tk, q=q):
                            h, n = sx_free[q % 2]
                            if h is not None:
                                _w(gpsimd, h, n)
                            Lsxp[q % 2].dinc(gpsimd.dma_start(
                                s_t[q % 2][:, :],
                                sd_d[tk * 512 + q * 128:
                                     tk * 512 + (q + 1) * 128, :]))
                            Lsxp[q % 2].dinc(gpsimd.dma_start(
                                x_t[q % 2][:, :],
                                x_d[tk * 512 + q * 128:
                                    tk * 512 + (q + 1) * 128, :]))
                        sx_here.append((Lsxp[q % 2], Lsxp[q % 2].n))

                    @block.vector
                    def _(vector, tk=tk, q=q):
                        _w(vector, S4, st4_sil[(q, 1)])
                        _w(vector, sx_here[q][0], sx_here[q][1])
                        ins = vector.tensor_tensor(
                            cmb[:, :], gts4[:, q * 1024:(q + 1) * 1024],
                            s_t[q % 2][:, :], ALU.mult)
                        V4.inc(ins)
                        vector.wait_ge(V4.h, V4.n)
                        ins = vector.tensor_tensor(s_t[q % 2][:, :],
                                                   cmb[:, :],
                                                   x_t[q % 2][:, :], ALU.add)
                        V4.inc(ins)
                        comb_cnt[q] = V4.n

                    @block.gpsimd
                    def _(gpsimd, tk=tk, q=q):
                        _w(gpsimd, V4, comb_cnt[q])
                        Gop[q % 2].dinc(gpsimd.dma_start(
                            out_d[tk * 512 + q * 128:tk * 512 + (q + 1) * 128, :],
                            s_t[q % 2][:, :]))
                    sx_free[q % 2] = (Gop[q % 2], Gop[q % 2].n)
                gts_free = (V4, V4.n)

                @block.scalar
                def _(scalar, tk=tk):
                    for g in range(8):
                        for o4 in range(4):
                            _w(scalar, P4, st1_mult[(g, o4, "y1")])
                            if g >= 1:
                                _w(scalar, V4, v4_mult[(tk, g - 1, o4)])
                            elif tk >= 1:
                                _w(scalar, V4, v4_mult[(tk - 1, 7, o4)])
                            ins = scalar.activation(tsil[o4][:, :],
                                                    ps8[o4][:, :], AF.Silu)
                            S4.inc(ins)
                assert S4.n == s4b1 + 32
                @block.scalar
                def _(scalar, tk=tk, hbw=hB_w_ok):
                    for stage, hout in ((0, hB), (1, hA)):
                        for g in range(4):
                            for o8 in range(8):
                                oc = g * 8 + o8
                                _w(scalar, P4, s23_mm[(tk, stage, g, o8)])
                                if stage == 0 and g == 0 and o8 == 0:
                                    h, n = hbw
                                    if h is not None:
                                        _w(scalar, h, n)
                                ins = scalar.activation(
                                    hout[:, oc * 512:(oc + 1) * 512],
                                    ps8[o8][:, :], AF.Gelu)
                                S4.inc(ins)
                assert S4.n == s4b32 + 64

                @block.scalar
                def _(scalar, tk=tk, gfree=gts_free0):
                    for q in range(4):
                        for od in range(2):
                            bank = q * 2 + od
                            _w(scalar, P4, st4_mm[bank])
                            if q == 0 and od == 0:
                                h, n = gfree
                                if h is not None:
                                    _w(scalar, h, n)
                            ins = scalar.activation(
                                gts4[:, q * 1024 + od * 512:
                                     q * 1024 + (od + 1) * 512],
                                ps8[bank][:, :], AF.Silu)
                            S4.inc(ins)
                assert S4.n == s4q + 8

                hA_w_ok = (P4, P4.n)
                hB_w_ok = (P4, P4.n)

            if debug:
                @block.gpsimd
                def _(gpsimd):
                    for sm in sems.values():
                        if sm.n > 0:
                            gpsimd.wait_ge(sm.h, sm.n)
                    Gp.dinc(gpsimd.dma_start(dbgh_d[:, :], hA[:, :]))
                    gpsimd.wait_ge(Gp.h, Gp.n)

            @block.gpsimd
            def _(gpsimd):
                gpsimd.wait_ge(Gop[0].h, Gop[0].n)
                gpsimd.wait_ge(Gop[1].h, Gop[1].n)

    return nc


_CACHE = {}


def kernel(L0_attn_x, L0_attn_wqk, L0_attn_bqk, L0_attn_gamma, L0_attn_beta,
           L0_attn_wv, L0_attn_bv, L0_attn_rel_bias,
           L0_ffn_w1_w, L0_ffn_w1_s, L0_ffn_w2_w, L0_ffn_w2_s,
           L0_ffn_h1_w, L0_ffn_h1_s, L0_ffn_h2_w, L0_ffn_h2_s,
           L0_ffn_w3_w, L0_ffn_w3_s):
    if "nc" not in _CACHE:
        _CACHE["nc"] = build_program()
    nc = _CACHE["nc"]

    f32 = np.float32
    common = {
        "wqk": np.ascontiguousarray(L0_attn_wqk, dtype=f32),
        "bqk": np.ascontiguousarray(L0_attn_bqk, dtype=f32),
        "gamma": np.ascontiguousarray(L0_attn_gamma, dtype=f32),
        "beta": np.ascontiguousarray(L0_attn_beta, dtype=f32),
        "wv": np.ascontiguousarray(L0_attn_wv, dtype=f32),
        "bv": np.ascontiguousarray(L0_attn_bv, dtype=f32),
        "relb": np.ascontiguousarray(L0_attn_rel_bias, dtype=f32),
        "emat": _emat_host(),
        "w1t": np.ascontiguousarray(np.asarray(L0_ffn_w1_w, dtype=f32).T),
        "w2t": np.ascontiguousarray(np.asarray(L0_ffn_w2_w, dtype=f32).T),
        "h1t": np.ascontiguousarray(np.asarray(L0_ffn_h1_w, dtype=f32).T),
        "h2t": np.ascontiguousarray(np.asarray(L0_ffn_h2_w, dtype=f32).T),
        "w3t": np.ascontiguousarray(np.asarray(L0_ffn_w3_w, dtype=f32).T),
        "w1s": np.ascontiguousarray(L0_ffn_w1_s, dtype=f32),
        "w2s": np.ascontiguousarray(L0_ffn_w2_s, dtype=f32),
        "h1s": np.ascontiguousarray(L0_ffn_h1_s, dtype=f32),
        "h2s": np.ascontiguousarray(L0_ffn_h2_s, dtype=f32),
        "w3s": np.ascontiguousarray(L0_ffn_w3_s, dtype=f32),
        "ident": np.eye(128).astype(ml_dtypes.bfloat16),
    }
    x = np.asarray(L0_attn_x, dtype=f32)
    in_maps = []
    for c in range(8):
        m = dict(common)
        m["x"] = np.ascontiguousarray(x[c])
        in_maps.append(m)

    res = run_bass_kernel_spmd(nc, in_maps, core_ids=list(range(8)))
    return np.stack([r["out"] for r in res.results], axis=0)
